# version 17
# baseline (speedup 1.0000x reference)
import contextlib
import numpy as np
import ml_dtypes
import concourse.bass as bass
import concourse.mybir as mybir
from concourse.bass_utils import run_bass_kernel_spmd

F32 = mybir.dt.float32
BF16 = mybir.dt.bfloat16
ACTF = mybir.ActivationFunctionType
ALU = mybir.AluOpType
AX = mybir.AxisListType
BF = ml_dtypes.bfloat16

EPS_LN = 1e-5
EPS_AT = 1e-6
ROPE_BASE = 10000.0


class Cfg:
    def __init__(self, D=2048, HQ=16, G=4, T=1024, PRE=1024, NE=16, ITR=1408,
                 CAP=256):
        self.CAP = CAP
        self.D = D
        self.HQ = HQ
        self.G = G
        self.HD = 128
        self.T = T
        self.PRE = PRE
        self.NE = NE
        self.ITR = ITR
        self.R = HQ // G
        self.DC = D // 128
        self.TC = T // 128
        self.NBP = PRE // 512
        self.NBO = T // 512
        self.NB = self.NBP + self.NBO
        self.BTC = 4
        self.ICN = ITR // 128
        self.OI = D // 512
        self.T2N = T // 512
        self.NU = NE + 2
        assert ITR % 128 == 0 and D % 512 == 0 and T % 512 == 0
        assert PRE % 512 == 0 and HQ % G == 0 and self.R * 128 <= 512


def build_nc(cfg, debug=()):
    c = cfg
    nc = bass.Bass("TRN2", target_bir_lowering=False)
    es = contextlib.ExitStack()

    d_x = nc.dram_tensor("x_own", [c.T, c.D], F32, kind="ExternalInput").ap()
    d_xp = nc.dram_tensor("x_pre", [c.PRE, c.D], F32, kind="ExternalInput").ap()
    d_cos = nc.dram_tensor("ctab", [128, c.NB * 512], F32, kind="ExternalInput").ap()
    d_sin = nc.dram_tensor("stab", [128, c.NB * 512], F32, kind="ExternalInput").ap()
    d_mask = nc.dram_tensor("maskT", [128, c.R * 128], F32, kind="ExternalInput").ap()
    d_id = nc.dram_tensor("identb", [128, 128], BF16, kind="ExternalInput").ap()
    d_on1 = nc.dram_tensor("ones1", [1, 128], F32, kind="ExternalInput").ap()
    d_vcol = nc.dram_tensor("vcol", [128, c.NB], F32, kind="ExternalInput").ap()
    d_wq = nc.dram_tensor("wq_st", [c.HQ, 128, c.D], F32, kind="ExternalInput").ap()
    d_wk = nc.dram_tensor("wk_st", [c.G, 128, c.D], F32, kind="ExternalInput").ap()
    d_wv = nc.dram_tensor("wv_st", [c.G, 128, c.D], F32, kind="ExternalInput").ap()
    d_wo = nc.dram_tensor("wo_mv", [c.OI, 128, c.DC * 512], F32, kind="ExternalInput").ap()
    d_gw = nc.dram_tensor("gw_mv", [128, c.DC * c.NE], F32, kind="ExternalInput").ap()
    d_w1 = nc.dram_tensor("w1_st", [c.NU, c.ICN, 128, c.D], F32, kind="ExternalInput").ap()
    d_w3 = nc.dram_tensor("w3_st", [c.NU, c.ICN, 128, c.D], F32, kind="ExternalInput").ap()
    d_w2 = nc.dram_tensor("w2_mv", [c.NU, c.OI, 128, c.ICN * 512], F32, kind="ExternalInput").ap()
    d_out = nc.dram_tensor("out", [c.T, c.D], F32, kind="ExternalOutput").ap()
    d_iotac = nc.dram_tensor("iotac", [16, c.NE * (c.CAP // 16)], F32,
                             kind="ExternalInput").ap()
    d_on16 = nc.dram_tensor("on16", [1, 16], F32, kind="ExternalInput").ap()
    d_tokp1 = nc.dram_tensor("tokp1", [16, c.NE * (c.T // 16)], F32,
                             kind="ExternalInput").ap()
    d_s1h = nc.dram_tensor("s1h", [c.T, c.D], BF16, kind="ExternalOutput").ap()
    d_wh = nc.dram_tensor("wh", [c.T, 64], F32, kind="ExternalInput").ap()
    d_yh = nc.dram_tensor("yh", [c.T, c.D], BF16, kind="ExternalOutput").ap()
    d_dbg = {}
    for name, shape, dt in debug:
        d_dbg[name] = nc.dram_tensor("dbg_" + name, list(shape), dt,
                                     kind="ExternalOutput").ap()

    P = [es.enter_context(nc.psum_tensor(f"P{i}", [128, 512], F32)) for i in range(7)]
    PB = es.enter_context(nc.psum_tensor("PBt", [128, 1024], BF16))

    class Sem:
        __slots__ = ("h", "n")
        def __init__(self, name):
            self.h = es.enter_context(nc.semaphore(name))
            self.n = 0
        def inc(self, instr, k=1):
            instr.then_inc(self.h, k)
            self.n += k
        def wait(self, eng, val=None):
            v = self.n if val is None else val
            if v > 0:
                eng.wait_ge(self.h, v)

    names = ["s_xm", "s_cst", "s_xtm0", "s_xtm1", "s_xtm2", "s_w4a", "s_w4b", "s_w4c",
             "s_w4d", "s_w2a", "s_w2b", "s_w2c", "s_shift", "s_out", "s_dbg",
             "s_tP", "s_tTx", "s_tTv", "s_tA", "s_tN", "s_tS", "s_tRB", "s_tWO",
             "s_tH", "s_tY", "s_tG", "s_tTm",
             "s_vQs", "s_vRope", "s_vXT", "s_vVf", "s_vVtm", "s_vAm", "s_vSnap",
             "s_vDiv", "s_vMir", "s_vHp", "s_vY", "s_vWo", "s_vGate", "s_vRec",
             "s_vLN", "s_tab", "s_tOT", "s_vOT", "s_s1h", "s_wh", "s_wr", "s_yz", "s_ga", "s_ga1", "s_gw2", "s_gw3", "s_sc", "s_sc1", "s_rep", "s_vLst", "s_gLst", "s_yb0", "s_yb1", "s_tB16",
             "s_cE", "s_cDen", "s_cRB", "s_cSilu", "s_cStd", "s_cCast", "s_gw"]
    S = {n: Sem(n) for n in names}

    sb = lambda st, name, shape, dt: st.enter_context(nc.sbuf_tensor(name, shape, dt))
    master = sb(es, "master", [128, c.TC * c.D], F32)
    ring4 = [sb(es, f"r4_{i}", [128, c.D], BF16) for i in range(3)]
    xtm = [sb(es, f"xtm_{i}", [128, c.D], BF16) for i in range(3)]
    maskT = sb(es, "maskT_s", [128, c.R * 128], F32)
    identb = sb(es, "identb_s", [128, 128], BF16)
    ones1 = sb(es, "ones1_s", [1, 128], F32)
    vcol = sb(es, "vcol_s", [128, c.NB], F32)
    S_acc = sb(es, "S_acc", [128, c.G * 129], F32)
    S_bf = sb(es, "S_bf", [128, c.G * 129], BF16)
    w_all = sb(es, "w_all", [128, c.TC * c.NE], F32)
    gwm = sb(es, "gwm_s", [128, c.DC * c.NE], BF16)
    red1 = sb(es, "red1", [128, 1], F32)
    red2 = sb(es, "red2", [128, 1], F32)
    rstd = sb(es, "rstd", [128, 1], F32)
    sqscr = sb(es, "sqscr", [128, 512], F32)
    bnbuf = sb(es, "bnbuf", [128, 6 * (c.D // 512)], F32)
    bnmv = sb(es, "bnmv", [128, 2], F32)
    g_sc = sb(es, "g_sc", [128, c.NE], F32)
    g_t = sb(es, "g_t", [128, c.NE], F32)
    g_m = sb(es, "g_m", [128, 1], F32)
    d_col = sb(es, "d_col", [128, 1], F32)
    d_col2 = sb(es, "d_col2", [128, 1], F32)

    nsync, ngp, nt, nv, nsc = nc.sync, nc.gpsimd, nc.tensor, nc.vector, nc.scalar

    P4b = P[4][:, :].bitcast(BF16)
    P56b = [P[5][:, :].bitcast(BF16), P[6][:, :].bitcast(BF16)]
    Pnb = [P[i][:, :].bitcast(BF16) for i in range(1, 5)]

    def tp_dst(i):
        return PB[:, 0:128] if i % 2 == 0 else P4b[:, 0:128]

    w4sem = [S["s_w4a"], S["s_w4b"], S["s_w4c"]]
    r4_hist = []
    consumed4 = []
    w4_idx = {"n": 0}

    def dma_ring4(src_ap):
        i = w4_idx["n"]
        w4_idx["n"] += 1
        if i >= 3:
            sem_h, val = consumed4[i - 3]
            ngp.wait_ge(sem_h, val)
        ins = ngp.dma_start(ring4[i % 3][:, 0:src_ap.shape[1]], src_ap)
        w4sem[i % 3].inc(ins, 16)
        r4_hist.append(w4sem[i % 3].n)
        return i

    def use_ring4(i):
        nt.wait_ge(w4sem[i % 3].h, r4_hist[i])
        return ring4[i % 3]

    w2sem = [S["s_w2a"], S["s_w2b"], S["s_w2c"]]
    r2 = {"ring": None, "n": 0, "hist": []}
    consumed2 = []

    def dma_ring2(src_ap):
        i = r2["n"]
        r2["n"] += 1
        if i >= 3:
            sem_h, val = consumed2[i - 3]
            ngp.wait_ge(sem_h, val)
        ins = ngp.dma_start(r2["ring"][i % 3][:, 0:src_ap.shape[1]], src_ap)
        w2sem[i % 3].inc(ins, 16)
        r2["hist"].append(w2sem[i % 3].n)
        return i

    def use_ring2(i):
        nt.wait_ge(w2sem[i % 3].h, r2["hist"][i])
        return r2["ring"][i % 3]

    att = contextlib.ExitStack()
    att2 = contextlib.ExitStack()
    aoutT = att.enter_context(
        nc.sbuf_tensor("aoutT", [128, c.HQ * c.T], BF16, side="right"))
    ropeq = [sb(att2, f"ropeq{i}", [128, 512], F32) for i in range(2)]
    ropes = [sb(att2, f"ropes{i}", [128, 512], F32) for i in range(2)]
    tmp1 = [sb(att2, f"tmp1_{i}", [128, 512], F32) for i in range(2)]
    tmp2 = sb(att2, "tmp2", [128, 512], F32)
    linb = sb(att2, "linb", [128, 512], F32)
    eexb = sb(att2, "eexb", [128, 512], F32)
    A_sb = [sb(att2, f"A_sb{i}", [128, c.R * 128], BF16) for i in range(2)]
    KfT_sb = [sb(att2, f"KfT{i}", [128, 128], BF16) for i in range(2)]
    Vf_sb = sb(att2, "Vf_sb", [128, 512], BF16)
    ot_tm = [sb(att2, f"ot_tm{i}", [128, c.R * 128], BF16) for i in range(2)]

    rope_j = {"n": 0}
    rope_hist = {}
    rope_pend = []

    def rope_s1(ps, ctab, stab, tab_col, dst_ap, w):
        j = rope_j["n"]
        rope_j["n"] += 1
        par = j % 2
        q_sb, q_sh = ropeq[par], ropes[par]
        h = {}
        S["s_shift"].wait(nv)
        nv.drain()
        nv.tensor_copy(q_sb[:, 0:w], ps[:, 0:w])
        nv.drain()
        ins = nv.tensor_tensor(tmp1[par][:, 0:w], q_sb[:, 0:w],
                               ctab[:, tab_col:tab_col + w], ALU.mult)
        S["s_vQs"].inc(ins, 1)
        h["qs"] = S["s_vQs"].n
        if j >= 2:
            nsc.wait_ge(S["s_vRope"].h, rope_hist[j - 2]["qr"])
        nsc.wait_ge(S["s_vQs"].h, h["qs"])
        ins = nsc.dma_start(q_sh[0:64, 0:w], q_sb[64:128, 0:w])
        S["s_shift"].inc(ins, 16)
        ins = nsc.dma_start(q_sh[64:128, 0:w], q_sb[0:64, 0:w])
        S["s_shift"].inc(ins, 16)
        h["sh"] = S["s_shift"].n
        nv.wait_ge(S["s_shift"].h, h["sh"])
        nv.drain()
        nv.tensor_tensor(tmp2[:, 0:w], q_sh[:, 0:w],
                         stab[:, tab_col:tab_col + w], ALU.mult)
        nv.drain()
        ins = nv.tensor_tensor(tmp1[par][:, 0:w], tmp1[par][:, 0:w],
                               tmp2[:, 0:w], ALU.add)
        S["s_vRope"].inc(ins, 1)
        h["qr"] = S["s_vRope"].n
        rope_hist[j] = h
        rope_pend.append((j, dst_ap, w))
        return j

    def rope_s2(j):
        par = j % 2
        w = rope_pend_w(j)
        nsc.wait_ge(S["s_vRope"].h, rope_hist[j]["qr"])
        if j >= 1 and "s3" in rope_hist.get(j - 1, {}):
            nsc.wait_ge(S["s_vRope"].h, rope_hist[j - 1]["s3"])
        nsc.drain()
        nsc.activation(linb[:, 0:512], tmp1[par][:, 0:512], ACTF.Relu)
        nsc.drain()
        nsc.activation(eexb[:, 0:512], tmp1[par][:, 0:512], ACTF.Relu,
                       scale=-1.0)
        nsc.drain()
        ins = nsc.activation(eexb[:, 0:512], eexb[:, 0:512], ACTF.Exp,
                             scale=-1.0)
        S["s_cE"].inc(ins, 1)
        rope_hist[j]["ce"] = S["s_cE"].n

    def rope_pend_w(j):
        return 512

    def rope_s3(j, dst_ap, w):
        nv.wait_ge(S["s_cE"].h, rope_hist[j]["ce"])
        nv.drain()
        ins = nv.tensor_tensor(dst_ap, eexb[:, 0:w], linb[:, 0:w], ALU.add)
        S["s_vRope"].inc(ins, 1)
        rope_hist[j]["s3"] = S["s_vRope"].n

    def rope_drain(upto=0):
        while len(rope_pend) > upto:
            j, dst_ap, w = rope_pend.pop(0)
            rope_s2(j)
            rope_s3(j, dst_ap, w)

    def rope_step():
        rope_drain(1)

    glob_u = {"n": 0}
    scan_u = {"n": 0}
    snap_g = {}
    rec = {}
    khist = {}
    tsu = {}
    sjh = {}

    for b in range(c.NB):
        is_pre = b < c.NBP
        blk = contextlib.ExitStack()
        xT_b = sb(blk, f"xT_{b}", [128, c.DC * 512], BF16)
        ctab_b = sb(blk, f"ctab_{b}", [128, 512], F32)
        stab_b = sb(blk, f"stab_{b}", [128, 512], F32)
        Kf_b = sb(blk, f"Kf_{b}", [128, c.G * 512], BF16)
        V_b = sb(blk, f"V_{b}", [128, c.BTC * c.G * 129], BF16)
        Qf_b = None if is_pre else sb(blk, f"Qf_{b}", [128, c.HQ * 512], BF16)
        V_b3 = V_b[:, :].rearrange("p (t g e) -> p t g e", t=c.BTC, g=c.G)

        with nc.Block() as _blk:
            @_blk.sync
            def _(_, b=b, is_pre=is_pre, xT_b=xT_b, Kf_b=Kf_b, V_b=V_b,
                  Qf_b=Qf_b, V_b3=V_b3, ctab_b=ctab_b, stab_b=stab_b):
                if b == 0:
                    for dst, src in [(maskT, d_mask), (identb, d_id),
                                     (ones1, d_on1), (vcol, d_vcol)]:
                        ins = nsync.dma_start(dst[:, :], src)
                        S["s_cst"].inc(ins, 16)
                    for tc in range(c.TC):
                        ins = nsync.dma_start(master[:, tc * c.D:(tc + 1) * c.D],
                                              d_x[tc * 128:(tc + 1) * 128, :])
                        S["s_xm"].inc(ins, 16)
                    ins = ngp.dma_start(gwm[:, :], d_gw)
                    S["s_gw"].inc(ins, 16)
                    ins = nv.memset(S_acc[:, :], 0.0)
                    S["s_vSnap"].inc(ins, 1)
                    S["s_cst"].wait(nv)
                    S["s_cst"].wait(nt)

                ins = nsync.dma_start(ctab_b[:, :], d_cos[:, b * 512:(b + 1) * 512])
                S["s_tab"].inc(ins, 16)
                ins = nsync.dma_start(stab_b[:, :], d_sin[:, b * 512:(b + 1) * 512])
                S["s_tab"].inc(ins, 16)
                S["s_tab"].wait(nv)

                xsrc = d_xp if is_pre else d_x
                row0 = (b if is_pre else b - c.NBP) * 512
                xtmsems = [S["s_xtm0"], S["s_xtm1"], S["s_xtm2"]]
                for tcb in range(c.BTC):
                    k = b * c.BTC + tcb
                    par = k % 3
                    if k >= 3:
                        ngp.wait_ge(S["s_tTx"].h, (k - 2) * c.DC)
                    ins = ngp.dma_start(
                        xtm[par][:, :],
                        xsrc[row0 + tcb * 128: row0 + (tcb + 1) * 128, :])
                    xtmsems[par].inc(ins, 16)
                    xtmsems[par].wait(nt)
                    for dc in range(c.DC):
                        tix = k * c.DC + dc
                        if tix >= 2:
                            nt.wait_ge(S["s_vXT"].h, tix - 1)
                        dst = tp_dst(tix)
                        ins = nt.transpose(dst,
                                           xtm[par][:, dc * 128:(dc + 1) * 128],
                                           identb[:, :])
                        S["s_tTx"].inc(ins, 1)
                        S["s_tTx"].wait(nv)
                        nv.drain()
                        ins = nv.tensor_copy(
                            xT_b[:, dc * 512 + tcb * 128: dc * 512 + (tcb + 1) * 128],
                            dst)
                        S["s_vXT"].inc(ins, 1)

                proj_units = [("k", g) for g in range(c.G)] + \
                             [("v", g) for g in range(c.G)] + \
                             ([] if is_pre else [("q", h) for h in range(c.HQ)])

                proj_free = []
                S["s_vXT"].wait(nt)
                projP = [P[0], P[1], P[2], P[3], P[5], P[6]]
                for iu, (kind, idx) in enumerate(proj_units):
                    src_d = {"k": d_wk, "v": d_wv, "q": d_wq}[kind]
                    si = dma_ring4(src_d[idx])
                    bank = projP[iu % 6]
                    if iu >= 6:
                        sem, val = proj_free[iu - 6]
                        nt.wait_ge(sem.h, val)
                    slab = use_ring4(si)
                    for dc in range(c.DC):
                        ins = nt.matmul(bank[:, :],
                                        slab[:, dc * 128:(dc + 1) * 128],
                                        xT_b[:, dc * 512:(dc + 1) * 512],
                                        start=(dc == 0), stop=(dc == c.DC - 1))
                    S["s_tP"].inc(ins, 1)
                    consumed4.append((S["s_tP"].h, S["s_tP"].n))
                    if kind == "v":
                        S["s_tP"].wait(nv)
                        nv.drain()
                        ins = nv.tensor_copy(Vf_sb[:, :], bank[:, :])
                        S["s_vVf"].inc(ins, 1)
                        proj_free.append((S["s_vVf"], S["s_vVf"].n))
                        S["s_vVf"].wait(nt)
                        for tcb in range(c.BTC):
                            vtix = S["s_tTv"].n
                            if vtix >= 2:
                                nt.wait_ge(S["s_vVtm"].h, S["s_vVtm"].n - 1)
                            dst = tp_dst(vtix)
                            ins = nt.transpose(dst,
                                               Vf_sb[:, tcb * 128:(tcb + 1) * 128],
                                               identb[:, :])
                            S["s_tTv"].inc(ins, 1)
                            S["s_tTv"].wait(nv)
                            nv.drain()
                            ins = nv.tensor_copy(V_b3[:, tcb, idx, 0:128], dst)
                            S["s_vVtm"].inc(ins, 1)
                    else:
                        S["s_tP"].wait(nv)
                        dst = (Kf_b if kind == "k" else Qf_b)[
                            :, idx * 512:(idx + 1) * 512]
                        rope_s1(bank, ctab_b, stab_b, 0, dst, 512)
                        proj_free.append((S["s_vQs"], S["s_vQs"].n))
                        rope_step()

                for tcb in range(c.BTC):
                    nv.drain()
                    ins = nv.tensor_copy(
                        V_b3[:, tcb, :, 128:129],
                        vcol[:, b:b + 1].broadcast_to((128, c.G, 1)))
                    S["s_vVtm"].inc(ins, 1)

                rope_drain(0)
                S["s_vRope"].wait(nt)
                S["s_vVtm"].wait(nt)

                Qf3 = None if is_pre else Qf_b[:, :].rearrange(
                    "p (h t) -> p h t", h=c.HQ)
                W = c.R * 128

                def sA(su, g, tcb):
                    nt.wait_ge(S["s_vAm"].h, su)
                    qmove = Qf3[:, g * c.R:(g + 1) * c.R,
                                tcb * 128:(tcb + 1) * 128]
                    ins = nt.matmul(P[0][:, 0:W],
                                    Kf_b[:, g * 512 + tcb * 128:
                                         g * 512 + (tcb + 1) * 128],
                                    qmove, start=True, stop=True)
                    S["s_tA"].inc(ins, 1)
                    rec[su] = {"tah": S["s_tA"].n}

                def sB(su, g, tcb):
                    nv.wait_ge(S["s_tA"].h, rec[su]["tah"])
                    if su - 2 in rec and "tnh" in rec[su - 2]:
                        nv.wait_ge(S["s_tN"].h, rec[su - 2]["tnh"])
                    nv.drain()
                    ins = nv.tensor_tensor(A_sb[su % 2][:, 0:W], P[0][:, 0:W],
                                           maskT[:, 0:W], ALU.mult)
                    S["s_vAm"].inc(ins, 1)
                    rec[su]["amh"] = S["s_vAm"].n

                def sC(su, g, tcb):
                    nt.wait_ge(S["s_vAm"].h, rec[su]["amh"])
                    nt.wait_ge(S["s_vSnap"].h, snap_g.get(g, 1))
                    if su - 2 in rec and "evh" in rec[su - 2]:
                        nt.wait_ge(S["s_vOT"].h, rec[su - 2]["evh"])
                    pair = (P[1], P[2]) if su % 2 == 0 else (P[3], P[4])
                    for r in range(c.R):
                        bank = pair[r // 2]
                        off = (r % 2) * 129
                        h = g * c.R + r
                        nt.matmul(bank[:, off:off + 129],
                                  A_sb[su % 2][:, r * 128:(r + 1) * 128],
                                  V_b3[:, tcb, g, 0:129],
                                  start=True, stop=False)
                        ins = nt.matmul(bank[:, off:off + 129],
                                        Qf3[:, h, tcb * 128:(tcb + 1) * 128],
                                        S_bf[:, g * 129:(g + 1) * 129],
                                        start=False, stop=True)
                    S["s_tN"].inc(ins, 1)
                    rec[su]["tnh"] = S["s_tN"].n

                def sD(su, g, tcb):
                    nv.wait_ge(S["s_tN"].h, rec[su]["tnh"])
                    if su - 2 in rec and "oth" in rec[su - 2]:
                        nv.wait_ge(S["s_tOT"].h, rec[su - 2]["oth"])
                    pair = (P[1], P[2]) if su % 2 == 0 else (P[3], P[4])
                    for r in range(c.R):
                        bank = pair[r // 2]
                        off = (r % 2) * 129
                        nv.drain()
                        nv.tensor_scalar(d_col[:, :],
                                         bank[:, off + 128:off + 129],
                                         EPS_AT, None, ALU.add)
                        nv.drain()
                        nv.reciprocal(d_col2[:, :], d_col[:, :])
                        nv.drain()
                        ins = nv.tensor_scalar(
                            ot_tm[su % 2][:, r * 128:(r + 1) * 128],
                            bank[:, off:off + 128],
                            d_col2[:, :], None, ALU.mult)
                        S["s_vDiv"].inc(ins, 1)
                    rec[su]["dvh"] = S["s_vDiv"].n

                def sEF(su, g, tcb):
                    nt.wait_ge(S["s_vDiv"].h, rec[su]["dvh"])
                    if su - 2 in rec and "evh" in rec[su - 2]:
                        nt.wait_ge(S["s_vOT"].h, rec[su - 2]["evh"])
                    t0 = (b - c.NBP) * 512 + tcb * 128
                    for r in range(c.R):
                        pb = Pnb[((su % 2) * 2) + (r % 2)]
                        treg = pb[:, 516 + (r // 2) * 128:
                                  516 + (r // 2) * 128 + 128]
                        if r >= 2:
                            nt.wait_ge(S["s_vOT"].h, S["s_vOT"].n - 1)
                        ins = nt.transpose(
                            treg, ot_tm[su % 2][:, r * 128:(r + 1) * 128],
                            identb[:, :])
                        S["s_tOT"].inc(ins, 1)
                        nv.wait_ge(S["s_tOT"].h, S["s_tOT"].n)
                        nv.drain()
                        h = g * c.R + r
                        ins = nv.tensor_copy(
                            aoutT[:, h * c.T + t0: h * c.T + t0 + 128], treg)
                        S["s_vOT"].inc(ins, 1)
                    rec[su]["oth"] = S["s_tOT"].n
                    rec[su]["evh"] = S["s_vOT"].n

                def stA(u, g, tcb):
                    parU = u % 2
                    tdst = P56b[parU][:, 512:640]
                    if u - 2 in sjh:
                        nt.wait_ge(S["s_vSnap"].h, sjh[u - 2])
                    ins = nt.transpose(tdst,
                                       Kf_b[:, g * 512 + tcb * 128:
                                            g * 512 + (tcb + 1) * 128],
                                       identb[:, :])
                    S["s_tS"].inc(ins, 1)
                    nv.wait_ge(S["s_tS"].h, S["s_tS"].n)
                    if u - 2 in tsu:
                        nv.wait_ge(S["s_tS"].h, tsu[u - 2])
                    nv.drain()
                    ins = nv.tensor_copy(KfT_sb[parU][:, :], tdst)
                    S["s_vSnap"].inc(ins, 1)
                    khist[u] = S["s_vSnap"].n

                def stB(u, g, tcb):
                    parU = u % 2
                    nt.wait_ge(S["s_vSnap"].h, khist[u])
                    ins = nt.matmul(P[5 + parU][:, 0:129],
                                    KfT_sb[parU][:, :],
                                    V_b3[:, tcb, g, 0:129],
                                    start=True, stop=True)
                    S["s_tS"].inc(ins, 1)
                    tsu[u] = S["s_tS"].n
                    nv.wait_ge(S["s_tS"].h, tsu[u])
                    nv.drain()
                    nv.tensor_tensor(S_acc[:, g * 129:(g + 1) * 129],
                                     S_acc[:, g * 129:(g + 1) * 129],
                                     P[5 + parU][:, 0:129], ALU.add)
                    nv.drain()
                    ins = nv.tensor_copy(S_bf[:, g * 129:(g + 1) * 129],
                                         S_acc[:, g * 129:(g + 1) * 129])
                    S["s_vSnap"].inc(ins, 1)
                    sjh[u] = S["s_vSnap"].n
                    snap_g[g] = S["s_vSnap"].n

                units = []
                for tcb in range(c.BTC):
                    for g in range(c.G):
                        u = glob_u["n"]
                        glob_u["n"] += 1
                        if not is_pre:
                            su = scan_u["n"]
                            scan_u["n"] += 1
                        else:
                            su = None
                        units.append((su, g, tcb, u))

                if is_pre:
                    for i in range(len(units) + 1):
                        if i < len(units):
                            _, g, tcb, u = units[i]
                            stA(u, g, tcb)
                        if i >= 1:
                            _, g, tcb, u = units[i - 1]
                            stB(u, g, tcb)
                else:
                    n_u = len(units)
                    for i in range(n_u + 3):
                        if i < n_u:
                            su, g, tcb, u = units[i]
                            sA(su, g, tcb)
                        if 1 <= i < n_u + 1:
                            su, g, tcb, u = units[i - 1]
                            sB(su, g, tcb)
                            sC(su, g, tcb)
                        if 2 <= i < n_u + 2:
                            su, g, tcb, u = units[i - 2]
                            sD(su, g, tcb)
                            sEF(su, g, tcb)
                            stA(u, g, tcb)
                            stB(u, g, tcb)

                if b == c.NB - 1:
                    if "qf" in d_dbg and Qf_b is not None:
                        S["s_vRope"].wait(nsync)
                        ins = nsync.dma_start(d_dbg["qf"], Qf_b[:, :])
                        S["s_dbg"].inc(ins, 16)
                        S["s_dbg"].wait(nsync)
                    if "kf" in d_dbg:
                        S["s_vRope"].wait(nsync)
                        ins = nsync.dma_start(d_dbg["kf"], Kf_b[:, :])
                        S["s_dbg"].inc(ins, 16)
                        S["s_dbg"].wait(nsync)
                    if "vb" in d_dbg:
                        S["s_vVtm"].wait(nsync)
                        ins = nsync.dma_start(d_dbg["vb"], V_b[:, :])
                        S["s_dbg"].inc(ins, 16)
                        S["s_dbg"].wait(nsync)
                    if "sacc" in d_dbg:
                        S["s_vSnap"].wait(nsync)
                        ins = nsync.dma_start(d_dbg["sacc"], S_acc[:, :])
                        S["s_dbg"].inc(ins, 16)
                        S["s_dbg"].wait(nsync)

        blk.close()

    if "aout" in d_dbg:
        with nc.Block() as _b:
            @_b.sync
            def _(_):
                S["s_vDiv"].wait(nsync)
                S["s_vOT"].wait(nsync)
                ins = nsync.dma_start(d_dbg["aout"], aoutT[:, :])
                S["s_dbg"].inc(ins, 16)
                S["s_dbg"].wait(nsync)

    att2.close()

    wob = contextlib.ExitStack()
    r2w = max(c.ICN, min(c.HQ, 8)) * 512
    r2["ring"] = [sb(wob, f"r2_{i}", [128, r2w], BF16) for i in range(3)]
    mirror = sb(wob, "mirror", [128, c.DC * c.T], BF16)

    with nc.Block() as _blk:
        @_blk.sync
        def _(_):
            S["s_vOT"].wait(nt)
            S["s_xm"].wait(nv)

            def emit_mirror_tp(tc):
                S["s_cCast"].wait(nt)
                for dc in range(c.DC):
                    tix = tc * c.DC + dc
                    if tix >= 2:
                        nt.wait_ge(S["s_vMir"].h, tix - 1)
                    dst = tp_dst(tix)
                    ins = nt.transpose(dst,
                                       xtm[tc % 3][:, dc * 128:(dc + 1) * 128],
                                       identb[:, :])
                    S["s_tTm"].inc(ins, 1)
                    S["s_tTm"].wait(nv)
                    nv.drain()
                    ins = nv.tensor_copy(
                        mirror[:, dc * c.T + tc * 128: dc * c.T + (tc + 1) * 128],
                        dst)
                    S["s_vMir"].inc(ins, 1)

            for oi in range(c.OI):
                h8 = min(c.HQ, 8)
                nhalf = c.HQ // h8
                sis = [dma_ring2(d_wo[oi][:, hh * h8 * 512:(hh + 1) * h8 * 512])
                       for hh in range(nhalf)]
                slabs = [use_ring2(s) for s in sis]
                for tc in range(c.TC):
                    bank = P[tc % 4]
                    if oi * c.TC + tc >= 4:
                        nt.wait_ge(S["s_vWo"].h, S["s_vWo"].n - 3)
                    for fc in range(c.HQ):
                        ins = nt.matmul(
                            bank[:, :],
                            aoutT[:, fc * c.T + tc * 128: fc * c.T + (tc + 1) * 128],
                            slabs[fc // h8][:, (fc % h8) * 512:(fc % h8 + 1) * 512],
                            start=(fc == 0), stop=(fc == c.HQ - 1))
                    S["s_tWO"].inc(ins, 1)
                    if tc == c.TC - 1:
                        for _hh in range(nhalf):
                            consumed2.append((S["s_tWO"].h, S["s_tWO"].n))
                    S["s_tWO"].wait(nv)
                    nv.drain()
                    msl = master[:, tc * c.D + oi * 512: tc * c.D + (oi + 1) * 512]
                    ins = nv.tensor_tensor(msl, msl, bank[:, :], ALU.add)
                    S["s_vWo"].inc(ins, 1)
                    if oi == c.OI - 1:
                        msl = master[:, tc * c.D:(tc + 1) * c.D]
                        nv.drain()
                        for gi in range(c.D // 512):
                            nv.bn_stats(bnbuf[:, gi * 6:(gi + 1) * 6],
                                        msl[:, gi * 512:(gi + 1) * 512])
                            nv.drain()
                        nv.bn_aggr(bnmv[:, :], bnbuf[:, :])
                        nv.drain()
                        nv.tensor_scalar(red2[:, :], bnmv[:, 1:2], 1.0, EPS_LN,
                                         ALU.mult, ALU.add)
                        nv.drain()
                        ins = nv.reciprocal(rstd[:, :], red2[:, :])
                        S["s_vLN"].inc(ins, 1)
                        S["s_vLN"].wait(nsc)
                        nsc.drain()
                        ins = nsc.activation(rstd[:, :], rstd[:, :], ACTF.Sqrt)
                        S["s_cStd"].inc(ins, 1)
                        S["s_cStd"].wait(nv)
                        nv.drain()
                        ins = nv.tensor_scalar(msl, msl, bnmv[:, 0:1], rstd[:, :],
                                               ALU.subtract, ALU.mult)
                        S["s_vLN"].inc(ins, 1)
                        S["s_vLN"].wait(nsc)
                        nsc.drain()
                        if tc >= 3:
                            nsc.wait_ge(S["s_tTm"].h, (tc - 2) * c.DC)
                            nsc.wait_ge(S["s_s1h"].h, 16 * (tc - 2))
                        ins = nsc.copy(xtm[tc % 3][:, :], msl)
                        S["s_cCast"].inc(ins, 1)
                        S["s_cCast"].wait(nsync)
                        S["s_s1h"].wait(nsync)
                        ins = nsync.dma_start(
                            d_s1h[tc * 128:(tc + 1) * 128, :], xtm[tc % 3][:, :])
                        S["s_s1h"].inc(ins, 16)
                        if tc >= 1:
                            emit_mirror_tp(tc - 1)
            emit_mirror_tp(c.TC - 1)

    if "r1s" in d_dbg:
        with nc.Block() as _b:
            @_b.sync
            def _(_):
                S["s_vLN"].wait(nsync)
                ins = nsync.dma_start(d_dbg["r1s"], master[:, :])
                S["s_dbg"].inc(ins, 16)
                S["s_dbg"].wait(nsync)
    if "mir" in d_dbg:
        with nc.Block() as _b:
            @_b.sync
            def _(_):
                S["s_vMir"].wait(nsync)
                ins = nsync.dma_start(d_dbg["mir"], mirror[:, :])
                S["s_dbg"].inc(ins, 16)
                S["s_dbg"].wait(nsync)

    att.close()

    ffb = contextlib.ExitStack()
    hp = sb(ffb, "hp", [128, c.ICN * 512], BF16)
    t1b = [sb(ffb, f"t1_{i}", [128, 512], F32) for i in range(2)]
    gx = [sb(ffb, f"gx_{i}", [128, c.DC * c.CAP], BF16) for i in range(2)]
    wg = [sb(ffb, f"wg_{i}", [128, (c.CAP // 128) * 64], F32) for i in range(2)]
    _nsc = 2 if c.OI >= 2 else 1
    _wsc = (c.OI // _nsc) * 512
    ysbq = [sb(ffb, f"ysb_{i}", [128, (c.CAP // 128) * _wsc], BF16)
            for i in range(2)]
    _TW = c.T // 16
    _CWE = c.NE * (c.CAP // 16)
    gxf0 = gx[0][0:16, :].bitcast(F32)
    gxf1 = gx[1][0:16, :].bitcast(F32)
    assert 2 * c.NE * _TW <= gxf0.shape[1]
    s_wwe = gxf0[:, 0:c.NE * _TW]
    s_arr = gxf0[:, c.NE * _TW:2 * c.NE * _TW]
    s_cmp = gxf1[:, 0:c.NE * _TW]
    s_tok = gxf1[:, c.NE * _TW:2 * c.NE * _TW]
    s_msk = gxf0[:, c.NE * _TW:c.NE * _TW + _CWE]
    s_san = gxf0[:, 0:_CWE]
    s_iotc = sb(ffb, "s_iotc", [16, _CWE], F32)
    s_on16 = sb(ffb, "s_on16", [1, 16], F32)
    s_nfa = sb(ffb, "s_nfa", [1, c.NE], mybir.dt.uint32)
    s_nff = sb(ffb, "s_nff", [1, c.NE], F32)
    s_nfi = sb(ffb, "s_nfi", [1, c.NE], mybir.dt.int32)
    s_nf16 = sb(ffb, "s_nf16", [16, c.NE], F32)
    s_idxa = sb(ffb, "s_idxa", [128, c.NE * (c.CAP // 16)], mybir.dt.int16)
    rnum = ffb.enter_context(nc.gpsimd.register("rnum"))
    rnum2 = ffb.enter_context(nc.gpsimd.register("rnum2"))
    rnum3 = ffb.enter_context(nc.gpsimd.register("rnum3"))
    rg2 = [rnum, rnum2, rnum3]
    from concourse import library_config as _LC

    CW = c.CAP // 16
    NCH = c.CAP // 128
    N_SC = _nsc
    OSC = c.OI // N_SC
    W_SC = OSC * 512

    with nc.Block() as _blk:
        @_blk.sync
        def _(_):
            ins = ngp.memset(gx[0][:, :], 0.0)
            ins = ngp.memset(gx[1][:, :], 0.0)
            ins = ngp.memset(wg[0][:, :], 0.0)
            ins = ngp.memset(wg[1][:, :], 0.0)
            ngp.drain()
            S["s_gLst"].inc(ins, 1)
            gx_init = S["s_gLst"].n
            ins = nv.memset(sqscr[:, :], 0.0)
            S["s_vLst"].inc(ins, 1)
            S["s_vLst"].wait(nsync)
            sqb = sqscr[:, :].bitcast(BF16)
            zw = min(sqb.shape[1], c.D)
            sqb = sqb[:, 0:zw]
            for tc in range(c.TC):
                for oz in range(c.D // zw):
                    ins = nsync.dma_start(
                        d_yh[tc * 128:(tc + 1) * 128, oz * zw:(oz + 1) * zw],
                        sqb)
                    S["s_yz"].inc(ins, 16)
            nsc.wait_ge(S["s_gLst"].h, gx_init)
            ins = nsc.dma_start(s_tok[:, :], d_tokp1)
            S["s_wr"].inc(ins, 16)
            ins = nsc.dma_start(s_iotc[:, :], d_iotac)
            S["s_wr"].inc(ins, 16)
            ins = nsc.dma_start(s_on16[:, :], d_on16)
            S["s_wr"].inc(ins, 16)

            S["s_vMir"].wait(nt)
            S["s_gw"].wait(nt)
            for tc in range(c.TC):
                for dc in range(c.DC):
                    ins = nt.matmul(P[4][:, tc * c.NE:(tc + 1) * c.NE],
                                    mirror[:, dc * c.T + tc * 128:
                                           dc * c.T + (tc + 1) * 128],
                                    gwm[:, dc * c.NE:(dc + 1) * c.NE],
                                    start=(dc == 0), stop=(dc == c.DC - 1))
            S["s_tG"].inc(ins, 1)
            for tc in range(c.TC):
                bank = P[4]
                S["s_tG"].wait(nv)
                nv.drain()
                nv.tensor_reduce(g_m[:, :], bank[:, tc * c.NE:(tc + 1) * c.NE],
                                 AX.X, ALU.max)
                nv.drain()
                ins = nv.tensor_scalar(g_sc[:, :],
                                       bank[:, tc * c.NE:(tc + 1) * c.NE],
                                       g_m[:, :], None, ALU.subtract)
                S["s_vGate"].inc(ins, 1)
                S["s_vGate"].wait(nsc)
                nsc.drain()
                ins = nsc.activation(g_sc[:, :], g_sc[:, :], ACTF.Exp)
                S["s_cSilu"].inc(ins, 1)
                S["s_cSilu"].wait(nv)
                nv.drain()
                nv.tensor_reduce(g_m[:, :], g_sc[:, :], AX.X, ALU.add)
                nv.drain()
                nv.reciprocal(g_m[:, :], g_m[:, :])
                nv.drain()
                nv.tensor_scalar(g_sc[:, :], g_sc[:, :], g_m[:, :], None, ALU.mult)
                nv.drain()
                nv.tensor_reduce(g_m[:, :], g_sc[:, :], AX.X, ALU.max)
                nv.drain()
                nv.tensor_scalar(g_t[:, :], g_sc[:, :], g_m[:, :], None, ALU.is_ge)
                nv.drain()
                nv.scalar_tensor_tensor(g_t[:, :], g_t[:, :], -1e30, g_sc[:, :],
                                        ALU.mult, ALU.add)
                nv.drain()
                nv.tensor_reduce(g_m[:, :], g_t[:, :], AX.X, ALU.max)
                nv.drain()
                nv.tensor_scalar(g_t[:, :], g_sc[:, :], g_m[:, :], None, ALU.is_ge)
                nv.drain()
                ins = nv.tensor_tensor(w_all[:, tc * c.NE:(tc + 1) * c.NE],
                                       g_sc[:, :], g_t[:, :], ALU.mult)
                S["s_vGate"].inc(ins, 1)
                S["s_vGate"].wait(nsync)
                ins = nsync.dma_start(d_wh[tc * 128:(tc + 1) * 128, 0:c.NE],
                                      w_all[:, tc * c.NE:(tc + 1) * c.NE])
                S["s_wh"].inc(ins, 16)

            TW = c.T // 16
            S["s_vGate"].wait(nsync)
            nsync.wait_ge(S["s_gLst"].h, gx_init)
            for e in range(c.NE):
                for jj in range(8):
                    srcw = w_all[jj * 16:(jj + 1) * 16, :].rearrange(
                        "p (t e) -> p t e", e=c.NE)[:, :, e]
                    ins = nsync.dma_start(
                        s_wwe[:, e * TW + jj * c.TC:e * TW + (jj + 1) * c.TC],
                        srcw, allow_slow_non_contiguous=True)
                    S["s_wr"].inc(ins, 16)
            resh_wm = S["s_wr"].n
            lists_state = {}

            def hook_arr():
                nv.wait_ge(S["s_wr"].h, resh_wm)
                nv.drain()
                nv.tensor_scalar(s_arr[:, :], s_wwe[:, :], 0.0, None,
                                 ALU.is_gt)
                nv.drain()
                nv.tensor_tensor(s_arr[:, :], s_arr[:, :], s_tok[:, :],
                                 ALU.mult)
                nv.drain()
                ins = nv.tensor_scalar(s_arr[:, :], s_arr[:, :], -1.0, None,
                                       ALU.add)
                S["s_vLst"].inc(ins, 1)
                lists_state["arrv"] = S["s_vLst"].n

            def hook_sg():
                ngp.wait_ge(S["s_vLst"].h, lists_state["arrv"])
                nc.gpsimd.load_library(_LC.sparse_gather)
                for e in range(c.NE):
                    ins = ngp.sparse_gather(
                        s_cmp[:, e * TW:(e + 1) * TW],
                        s_arr[:, e * TW:(e + 1) * TW],
                        num_found=s_nfa[0:1, e:e + 1])
                S["s_gLst"].inc(ins, 1)
                nc.gpsimd.load_library(_LC.mlp)

            def hook_post():
                S["s_gLst"].wait(nv)
                nv.drain()
                nv.tensor_copy(s_nff[:, :], s_nfa[:, :])
                nv.drain()
                ins = nv.tensor_scalar(s_nff[:, :], s_nff[:, :],
                                       float(c.CAP), None, ALU.min)
                S["s_vLst"].inc(ins, 1)
                vmin = S["s_vLst"].n
                nt.wait_ge(S["s_vLst"].h, vmin)
                ins = nt.matmul(P[4][0:16, 0:c.NE], s_on16[:, :],
                                s_nff[0:1, :], start=True, stop=True)
                S["s_tB16"].inc(ins, 1)
                S["s_tB16"].wait(nv)
                nv.drain()
                nv.tensor_copy(s_nf16[:, :], P[4][0:16, 0:c.NE])
                nv.drain()
                for e in range(c.NE):
                    nv.tensor_scalar(s_msk[:, e * CW:(e + 1) * CW],
                                     s_iotc[:, e * CW:(e + 1) * CW],
                                     s_nf16[:, e:e + 1], None, ALU.is_lt)
                nv.drain()
                cmp3 = s_cmp[:, :].rearrange("p (e w) -> p e w",
                                             w=TW)[:, :, 0:CW]
                san3 = s_san[:, :].rearrange("p (e w) -> p e w", w=CW)
                nv.tensor_scalar(san3, cmp3, 1.0, None, ALU.add)
                nv.drain()
                nv.tensor_tensor(s_san[:, :], s_san[:, :], s_msk[:, :],
                                 ALU.mult)
                nv.drain()
                nv.tensor_scalar(s_san[:, :], s_san[:, :], -1.0, None, ALU.add)
                nv.drain()
                nv.tensor_copy(s_idxa[0:16, :], s_san[:, :])
                nv.drain()
                ins = nv.tensor_copy(s_nfi[:, :], s_nff[:, :])
                S["s_vLst"].inc(ins, 1)
                lists_state["iv"] = S["s_vLst"].n
                S["s_vLst"].wait(nsync)
                for rr in range(1, 8):
                    ins = nsync.dma_start(s_idxa[16 * rr:16 * (rr + 1), :],
                                          s_idxa[0:16, :])
                    S["s_rep"].inc(ins, 16)
                lists_state["rv"] = S["s_rep"].n

            arr_ic = min(6, c.ICN - 1)
            sg_oi = max(0, min(2, c.OI - 3))
            post_oi = max(sg_oi, min(6, c.OI - 2))
            ga_oi = max(post_oi, min(c.OI - 1, post_oi + 1))

            S["s_dbg"].wait(nv)
            qlast = [None, None]
            pend_sc = []
            hval = {}
            yval = {}
            gav = {}
            gwv = {}
            scv = {}

            def flush_sc(upto_oi=None):
                while pend_sc and (upto_oi is None
                                   or pend_sc[0][0] <= upto_oi):
                    pend_sc.pop(0)[1]()

            def issue_gathers(e):
                ngp.wait_ge(S["s_rep"].h, lists_state["rv"])
                ngp.wait_ge(S["s_vLst"].h, lists_state["iv"])
                S["s_s1h"].wait(ngp)
                S["s_wh"].wait(ngp)
                if e >= 2:
                    ngp.wait_ge(S["s_tH"].h, hval[e - 2])
                    ngp.wait_ge(S["s_vY"].h, yval[e - 2])
                ngp.reg_load(rg2[e % 3], s_nfi[0:1, e:e + 1])
                sga = S["s_ga"] if e % 2 == 0 else S["s_ga1"]
                sgw = S["s_gw2"] if e % 2 == 0 else S["s_gw3"]
                ins = ngp.dma_gather(
                    gx[e % 2][:, :].rearrange("p (d t) -> p d t", t=c.CAP),
                    d_s1h, s_idxa[:, e * CW:(e + 1) * CW], c.CAP, rg2[e % 3],
                    c.D, transpose=True)
                sga.inc(ins, 16)
                gav[e] = (sga.h, sga.n)
                ins = ngp.dma_gather(
                    wg[e % 2][:, :].rearrange("p (t e2) -> p t e2", e2=64),
                    d_wh, s_idxa[:, e * CW:(e + 1) * CW], c.CAP, rg2[e % 3],
                    64, transpose=False)
                sgw.inc(ins, 16)
                gwv[e] = (sgw.h, sgw.n)


            def queue_scatter(e, q, vyv):
                qb = q % 2
                ssc = S["s_sc"] if e % 2 == 0 else S["s_sc1"]

                def emit(e=e, q=q, qb=qb, vyv=vyv, ssc=ssc):
                    S["s_yz"].wait(ngp)
                    ngp.wait_ge(S["s_vY"].h, vyv)
                    ysb3 = ysbq[qb][:, :].rearrange("p (t d) -> p t d",
                                                    d=W_SC)
                    ins = ngp.dma_scatter_add(
                        d_yh[:, q * W_SC:(q + 1) * W_SC], ysb3[:, :, :],
                        s_idxa[:, e * CW:(e + 1) * CW], c.CAP, rg2[e % 3],
                        W_SC, elem_step=c.D)
                    ssc.inc(ins, 16)
                    scv[(e, q)] = (ssc.h, ssc.n)
                    qlast[qb] = (ssc.h, ssc.n)

                emit_oi = min((q + 1) * OSC + 5, (q + 2) * OSC - 1)
                if q == N_SC - 1:
                    emit_oi = 10 ** 9
                pend_sc.append((emit_oi, emit))

            pre4 = {}
            for u in range(c.NU):
                sparse = u >= 2
                e = u - 2
                if sparse and e + 1 < c.NE:
                    issue_gathers(e + 1)
                S["s_tY"].wait(nv)
                tspan = c.CAP if sparse else 512
                t2n = 1 if sparse else c.T2N
                for t2 in range(t2n):
                    S["s_tY"].wait(nv)
                    pre2 = {}
                    for ic in range(c.ICN):
                        if u == 1 and t2 == t2n - 1 and ic == arr_ic:
                            hook_arr()
                        if ic == max(0, c.ICN - 3):
                            pre2[0] = dma_ring2(d_w2[u, 0])
                        if ic == c.ICN - 1 and 0 in pre2 and c.OI > 1:
                            pre2[1] = dma_ring2(d_w2[u, 1])
                        if t2 == 0 and ic == 0 and u in pre4:
                            i3, i1 = pre4.pop(u)
                        else:
                            i3 = dma_ring4(d_w3[u, ic])
                            i1 = dma_ring4(d_w1[u, ic])
                        j = t2 * c.ICN + ic
                        p2 = j % 2
                        bh1, bh3 = P[0 + 2 * p2], P[1 + 2 * p2]
                        if u > 0 or j >= 2:
                            nt.wait_ge(S["s_cSilu"].h, S["s_cSilu"].n - 1)
                            nt.wait_ge(S["s_vHp"].h, S["s_vHp"].n - 1)
                        if sparse and ic == 0:
                            nt.wait_ge(*gav[e])
                        s3 = use_ring4(i3)
                        for dc in range(c.DC):
                            mov = (gx[e % 2][:, dc * c.CAP:(dc + 1) * c.CAP]
                                   if sparse else
                                   mirror[:, dc * c.T + t2 * 512:
                                          dc * c.T + (t2 + 1) * 512])
                            nt.matmul(bh3[:, 0:tspan],
                                      s3[:, dc * 128:(dc + 1) * 128],
                                      mov, start=(dc == 0), stop=(dc == c.DC - 1))
                        s1 = use_ring4(i1)
                        for dc in range(c.DC):
                            mov = (gx[e % 2][:, dc * c.CAP:(dc + 1) * c.CAP]
                                   if sparse else
                                   mirror[:, dc * c.T + t2 * 512:
                                          dc * c.T + (t2 + 1) * 512])
                            ins = nt.matmul(bh1[:, 0:tspan],
                                            s1[:, dc * 128:(dc + 1) * 128],
                                            mov, start=(dc == 0),
                                            stop=(dc == c.DC - 1))
                        S["s_tH"].inc(ins, 1)
                        consumed4.append((S["s_tH"].h, S["s_tH"].n))
                        consumed4.append((S["s_tH"].h, S["s_tH"].n))
                        S["s_tH"].wait(nsc)
                        nsc.drain()
                        nsc.wait_ge(S["s_vHp"].h, max(0, S["s_vHp"].n - 1))
                        ins = nsc.activation(t1b[p2][:, 0:tspan],
                                             bh1[:, 0:tspan], ACTF.Silu)
                        S["s_cSilu"].inc(ins, 1)
                        S["s_cSilu"].wait(nv)
                        nv.drain()
                        ins = nv.tensor_tensor(
                            hp[:, ic * tspan:(ic + 1) * tspan],
                            t1b[p2][:, 0:tspan], bh3[:, 0:tspan], ALU.mult)
                        S["s_vHp"].inc(ins, 1)
                    if sparse:
                        hval[e] = S["s_tH"].n

                    ntok = tspan // 128
                    if t2 == t2n - 1:
                        flush_sc()
                    for oi in range(c.OI):
                        if u == 1 and t2 == t2n - 1:
                            if oi == sg_oi:
                                hook_sg()
                            if oi == post_oi:
                                hook_post()
                            if oi == ga_oi:
                                issue_gathers(0)
                        if sparse:
                            flush_sc(upto_oi=oi)
                        if (t2 == t2n - 1 and oi == max(0, c.OI - 4)
                                and u + 1 < c.NU):
                            pre4[u + 1] = (dma_ring4(d_w3[u + 1, 0]),
                                           dma_ring4(d_w1[u + 1, 0]))
                        si = pre2[oi] if oi in pre2 else dma_ring2(d_w2[u, oi])
                        slab = use_ring2(si)
                        S["s_vHp"].wait(nt)
                        for tch in range(ntok):
                            yb = P[5 + (tch % 2)]
                            nt.wait_ge(S["s_vY"].h, max(0, S["s_vY"].n - 1))
                            for ic in range(c.ICN):
                                ins = nt.matmul(
                                    yb[:, :],
                                    hp[:, ic * tspan + tch * 128:
                                       ic * tspan + (tch + 1) * 128],
                                    slab[:, ic * 512:(ic + 1) * 512],
                                    start=(ic == 0), stop=(ic == c.ICN - 1))
                            S["s_tY"].inc(ins, 1)
                            if tch == ntok - 1:
                                consumed2.append((S["s_tY"].h, S["s_tY"].n))
                            S["s_tY"].wait(nv)
                            nv.drain()
                            if not sparse:
                                tc = t2 * 4 + tch
                                msl = master[:, tc * c.D + oi * 512:
                                             tc * c.D + (oi + 1) * 512]
                                ins = nv.tensor_tensor(msl, msl, yb[:, :],
                                                       ALU.add)
                            else:
                                qb = (oi // OSC) % 2
                                nv.wait_ge(*gwv[e])
                                if (oi % OSC == 0 and tch == 0
                                        and qlast[qb] is not None):
                                    nv.wait_ge(*qlast[qb])
                                ins = nv.tensor_scalar(
                                    ysbq[qb][:, tch * W_SC
                                             + (oi % OSC) * 512:
                                             tch * W_SC
                                             + (oi % OSC) * 512 + 512],
                                    yb[:, :],
                                    wg[e % 2][:, tch * 64 + e: tch * 64 + e + 1],
                                    None, ALU.mult)
                            S["s_vY"].inc(ins, 1)
                        if sparse and (oi + 1) % OSC == 0:
                            queue_scatter(e, oi // OSC, S["s_vY"].n)
                if sparse:
                    yval[e] = S["s_vY"].n

            flush_sc()
            S["s_sc"].wait(nsc)
            S["s_sc1"].wait(nsc)
            S["s_tH"].wait(nsc)
            ybv = {}
            for tc in range(c.TC):
                ybk = gx[tc % 2]
                if tc >= 2:
                    nsc.wait_ge(S["s_vY"].h, ybv[tc - 2])
                syb = S["s_yb0"] if tc % 2 == 0 else S["s_yb1"]
                ins = nsc.dma_start(ybk[:, 0:c.D],
                                    d_yh[tc * 128:(tc + 1) * 128, :])
                syb.inc(ins, 16)
                syb.wait(nv)
                nv.drain()
                msl = master[:, tc * c.D:(tc + 1) * c.D]
                ins = nv.tensor_tensor(msl, msl, ybk[:, 0:c.D], ALU.add)
                S["s_vY"].inc(ins, 1)
                ybv[tc] = S["s_vY"].n

            if "wal" in d_dbg:
                S["s_vGate"].wait(nsync)
                ins = nsync.dma_start(d_dbg["wal"], w_all[:, :])
                S["s_dbg"].inc(ins, 16)
                S["s_dbg"].wait(nsync)
            if "hp" in d_dbg:
                S["s_vHp"].wait(nsync)
                ins = nsync.dma_start(d_dbg["hp"], hp[:, :])
                S["s_dbg"].inc(ins, 16)
                S["s_dbg"].wait(nsync)

            for tc in range(c.TC):
                msl = master[:, tc * c.D:(tc + 1) * c.D]
                nv.drain()
                for gi in range(c.D // 512):
                    nv.bn_stats(bnbuf[:, gi * 6:(gi + 1) * 6],
                                msl[:, gi * 512:(gi + 1) * 512])
                    nv.drain()
                nv.bn_aggr(bnmv[:, :], bnbuf[:, :])
                nv.drain()
                nv.tensor_scalar(red2[:, :], bnmv[:, 1:2], 1.0, EPS_LN,
                                 ALU.mult, ALU.add)
                nv.drain()
                ins = nv.reciprocal(rstd[:, :], red2[:, :])
                S["s_vLN"].inc(ins, 1)
                S["s_vLN"].wait(nsc)
                nsc.drain()
                ins = nsc.activation(rstd[:, :], rstd[:, :], ACTF.Sqrt)
                S["s_cStd"].inc(ins, 1)
                S["s_cStd"].wait(nv)
                nv.drain()
                ins = nv.tensor_scalar(msl, msl, bnmv[:, 0:1], rstd[:, :],
                                       ALU.subtract, ALU.mult)
                S["s_vLN"].inc(ins, 1)
                S["s_vLN"].wait(nsync)
                ins = nsync.dma_start(d_out[tc * 128:(tc + 1) * 128, :], msl)
                S["s_out"].inc(ins, 16)
            S["s_out"].wait(nsync)
            S["s_dbg"].wait(nsync)

    ffb.close()
    wob.close()
    es.close()
    return nc


def _tile_stationary(W):
    O, D = W.shape
    return np.ascontiguousarray(
        W.reshape(O // 128, 128, D // 128, 128).transpose(0, 3, 2, 1)
        .reshape(O // 128, 128, D))


def _tile_moving(W, nt):
    O, I = W.shape
    return np.ascontiguousarray(
        W.reshape(O // nt, nt, I // 128, 128).transpose(0, 3, 2, 1)
        .reshape(O // nt, 128, (I // 128) * nt))


def _rope_tables(positions, hd=128, base=ROPE_BASE):
    inv = (1.0 / base ** (np.arange(0, hd, 2, dtype=np.float32)
                          / np.float32(hd))).astype(np.float32)
    freqs = np.outer(positions.astype(np.float32), inv)
    emb = np.concatenate([freqs, freqs], axis=-1)
    cos = np.cos(emb).astype(np.float32)
    sin = np.sin(emb).astype(np.float32)
    sgn = np.where(np.arange(hd) < hd // 2, np.float32(-1.0), np.float32(1.0))
    return (np.ascontiguousarray(cos.T),
            np.ascontiguousarray((sin * sgn[None, :]).T))


def host_inputs(cfg, n_cores, x, Wq, Wk, Wv, Wo, gate_w, w1, w2, w3,
                sw1, sw2, sw3):
    c = cfg
    wq_st = _tile_stationary(np.asarray(Wq))
    wk_st = _tile_stationary(np.asarray(Wk))
    wv_st = _tile_stationary(np.asarray(Wv))
    wo_mv = _tile_moving(np.asarray(Wo), 512)
    gwT = np.ascontiguousarray(np.asarray(gate_w).T)
    gw_mv = np.ascontiguousarray(
        gwT.reshape(c.DC, 128, c.NE).transpose(1, 0, 2).reshape(128, c.DC * c.NE))
    ITR = c.ITR
    w1u = [np.asarray(sw1)[0:ITR], np.asarray(sw1)[ITR:2 * ITR]] + \
          [np.asarray(w1)[e] for e in range(c.NE)]
    w3u = [np.asarray(sw3)[0:ITR], np.asarray(sw3)[ITR:2 * ITR]] + \
          [np.asarray(w3)[e] for e in range(c.NE)]
    w2u = [np.asarray(sw2)[:, 0:ITR], np.asarray(sw2)[:, ITR:2 * ITR]] + \
          [np.asarray(w2)[e] for e in range(c.NE)]
    w1_st = np.ascontiguousarray(np.stack([_tile_stationary(w) for w in w1u]))
    w3_st = np.ascontiguousarray(np.stack([_tile_stationary(w) for w in w3u]))
    w2_mv = np.ascontiguousarray(np.stack([_tile_moving(w, 512) for w in w2u]))
    maskT = np.ascontiguousarray(
        np.tile(np.tril(np.ones((128, 128), np.float32)).T, (1, c.R)))
    iotac = np.ascontiguousarray(np.tile(
        np.arange(c.CAP, dtype=np.float32).reshape(c.CAP // 16, 16).T,
        (1, c.NE)))
    on16 = np.ones((1, 16), np.float32)
    tokp1 = np.zeros((16, c.T // 16), np.float32)
    for jj in range(8):
        for tc in range(c.TC):
            tokp1[:, jj * c.TC + tc] = (tc * 128 + jj * 16 +
                                        np.arange(16) + 1.0)
    tokp1 = np.ascontiguousarray(np.tile(tokp1, (1, c.NE)))
    wh0 = np.zeros((c.T, 64), np.float32)
    identb = np.eye(128, dtype=np.float32).astype(BF)
    ones1 = np.ones((1, 128), np.float32)

    xf = np.asarray(x)
    B = xf.shape[0]
    halves = n_cores // B
    in_maps = []
    for core in range(n_cores):
        b, h = core // halves, core % halves
        x_own = np.ascontiguousarray(xf[b, h * c.T:(h + 1) * c.T, :])
        if h > 0:
            x_pre = np.ascontiguousarray(xf[b, (h - 1) * c.T:h * c.T, :]) \
                if c.PRE == c.T else np.ascontiguousarray(xf[b, 0:c.PRE, :])
            pre_flag = 1.0
        else:
            x_pre = np.zeros((c.PRE, c.D), np.float32)
            pre_flag = 0.0
        pos = np.zeros(c.NB * 512, np.float32)
        for blk in range(c.NB):
            if blk < c.NBP:
                pos[blk * 512:(blk + 1) * 512] = np.arange(blk * 512,
                                                           (blk + 1) * 512)
            else:
                off = h * c.T + (blk - c.NBP) * 512
                pos[blk * 512:(blk + 1) * 512] = np.arange(off, off + 512)
        ctab, stab = _rope_tables(pos, c.HD)
        vcol = np.zeros((128, c.NB), np.float32)
        vcol[:, 0:c.NBP] = pre_flag
        vcol[:, c.NBP:] = 1.0
        in_maps.append({
            "x_own": x_own, "x_pre": x_pre, "ctab": ctab, "stab": stab,
            "maskT": maskT, "identb": identb, "ones1": ones1, "vcol": vcol,
            "wq_st": wq_st, "wk_st": wk_st, "wv_st": wv_st, "wo_mv": wo_mv,
            "gw_mv": gw_mv, "w1_st": w1_st, "w3_st": w3_st, "w2_mv": w2_mv,
            "iotac": iotac, "on16": on16, "tokp1": tokp1, "wh": wh0,
        })
    return in_maps


def kernel(L0_attn_x, L0_attn_Wq, L0_attn_Wk, L0_attn_Wv, L0_attn_Wo,
           L0_ffn_gate_w, L0_ffn_w1, L0_ffn_w2, L0_ffn_w3,
           L0_ffn_sw1, L0_ffn_sw2, L0_ffn_sw3):
    cfg = Cfg()
    in_maps = host_inputs(cfg, 8, L0_attn_x, L0_attn_Wq, L0_attn_Wk,
                          L0_attn_Wv, L0_attn_Wo, L0_ffn_gate_w, L0_ffn_w1,
                          L0_ffn_w2, L0_ffn_w3, L0_ffn_sw1, L0_ffn_sw2,
                          L0_ffn_sw3)
    nc = build_nc(cfg)
    mybir.codegen_inst_isa_subclasses(nc)
    res = run_bass_kernel_spmd(nc, in_maps, core_ids=list(range(8)))
    out = np.empty((4, 2048, 2048), np.float32)
    for core in range(8):
        b, h = core // 2, core % 2
        out[b, h * cfg.T:(h + 1) * cfg.T, :] = res.results[core]["out"]
    return out


# version 19
# speedup vs baseline: 1.1872x; 1.1872x over previous
import contextlib
import numpy as np
import ml_dtypes
import concourse.bass as bass
import concourse.mybir as mybir
from concourse.bass_utils import run_bass_kernel_spmd

F32 = mybir.dt.float32
BF16 = mybir.dt.bfloat16
ACTF = mybir.ActivationFunctionType
ALU = mybir.AluOpType
AX = mybir.AxisListType
BF = ml_dtypes.bfloat16

EPS_LN = 1e-5
EPS_AT = 1e-6
ROPE_BASE = 10000.0


class Cfg:
    def __init__(self, D=2048, HQ=16, G=4, T=1024, PRE=1024, NE=16, ITR=1408,
                 CAP=256):
        self.CAP = CAP
        self.D = D
        self.HQ = HQ
        self.G = G
        self.HD = 128
        self.T = T
        self.PRE = PRE
        self.NE = NE
        self.ITR = ITR
        self.R = HQ // G
        self.DC = D // 128
        self.TC = T // 128
        self.NBP = PRE // 512
        self.NBO = T // 512
        self.NB = self.NBP + self.NBO
        self.BTC = 4
        self.ICN = ITR // 128
        self.OI = D // 512
        self.T2N = T // 512
        self.NU = NE + 2
        assert ITR % 128 == 0 and D % 512 == 0 and T % 512 == 0
        assert PRE % 512 == 0 and HQ % G == 0 and self.R * 128 <= 512


def build_nc(cfg, debug=()):
    c = cfg
    nc = bass.Bass("TRN2", target_bir_lowering=False)
    es = contextlib.ExitStack()

    d_x = nc.dram_tensor("x_own", [c.T, c.D], F32, kind="ExternalInput").ap()
    d_xp = nc.dram_tensor("x_pre", [c.PRE, c.D], F32, kind="ExternalInput").ap()
    d_cos = nc.dram_tensor("ctab", [128, c.NB * 512], F32, kind="ExternalInput").ap()
    d_sin = nc.dram_tensor("stab", [128, c.NB * 512], F32, kind="ExternalInput").ap()
    d_mask = nc.dram_tensor("maskT", [128, c.R * 128], F32, kind="ExternalInput").ap()
    d_id = nc.dram_tensor("identb", [128, 128], BF16, kind="ExternalInput").ap()
    d_on1 = nc.dram_tensor("ones1", [1, 128], F32, kind="ExternalInput").ap()
    d_vcol = nc.dram_tensor("vcol", [128, c.NB], F32, kind="ExternalInput").ap()
    d_wq = nc.dram_tensor("wq_st", [c.HQ, 128, c.D], F32, kind="ExternalInput").ap()
    d_wk = nc.dram_tensor("wk_st", [c.G, 128, c.D], F32, kind="ExternalInput").ap()
    d_wv = nc.dram_tensor("wv_st", [c.G, 128, c.D], F32, kind="ExternalInput").ap()
    d_wo = nc.dram_tensor("wo_mv", [c.OI, 128, c.DC * 512], F32, kind="ExternalInput").ap()
    d_gw = nc.dram_tensor("gw_mv", [128, c.DC * c.NE], F32, kind="ExternalInput").ap()
    d_w1 = nc.dram_tensor("w1_st", [c.NU, c.ICN, 128, c.D], F32, kind="ExternalInput").ap()
    d_w3 = nc.dram_tensor("w3_st", [c.NU, c.ICN, 128, c.D], F32, kind="ExternalInput").ap()
    d_w2 = nc.dram_tensor("w2_mv", [c.NU, c.OI, 128, c.ICN * 512], F32, kind="ExternalInput").ap()
    d_out = nc.dram_tensor("out", [c.T, c.D], F32, kind="ExternalOutput").ap()
    d_iotac = nc.dram_tensor("iotac", [16, c.NE * (c.CAP // 16)], F32,
                             kind="ExternalInput").ap()
    d_on16 = nc.dram_tensor("on16", [1, 16], F32, kind="ExternalInput").ap()
    d_tokp1 = nc.dram_tensor("tokp1", [16, c.NE * (c.T // 16)], F32,
                             kind="ExternalInput").ap()
    d_s1h = nc.dram_tensor("s1h", [c.T, c.D], BF16, kind="ExternalOutput").ap()
    d_wh = nc.dram_tensor("wh", [c.T, 64], F32, kind="ExternalInput").ap()
    d_yh = nc.dram_tensor("yh", [c.T, c.D], BF16, kind="ExternalOutput").ap()
    d_dbg = {}
    for name, shape, dt in debug:
        d_dbg[name] = nc.dram_tensor("dbg_" + name, list(shape), dt,
                                     kind="ExternalOutput").ap()

    P = [es.enter_context(nc.psum_tensor(f"P{i}", [128, 512], F32)) for i in range(7)]
    PB = es.enter_context(nc.psum_tensor("PBt", [128, 1024], BF16))

    class Sem:
        __slots__ = ("h", "n")
        def __init__(self, name):
            self.h = es.enter_context(nc.semaphore(name))
            self.n = 0
        def inc(self, instr, k=1):
            instr.then_inc(self.h, k)
            self.n += k
        def wait(self, eng, val=None):
            v = self.n if val is None else val
            if v > 0:
                eng.wait_ge(self.h, v)

    names = ["s_xm", "s_cst", "s_xtm0", "s_xtm1", "s_xtm2", "s_w4a", "s_w4b", "s_w4c",
             "s_w4d", "s_w4e", "s_w2a", "s_w2b", "s_w2c", "s_shift", "s_out", "s_dbg",
             "s_tP", "s_tTx", "s_tTv", "s_tA", "s_tN", "s_tS", "s_tRB", "s_tWO",
             "s_tH", "s_tY", "s_tG", "s_tTm",
             "s_vQs", "s_vRope", "s_vXT", "s_vVf", "s_vVtm", "s_vAm", "s_vSnap",
             "s_vDiv", "s_vMir", "s_vHp", "s_vY", "s_vWo", "s_vGate", "s_vRec",
             "s_vLN", "s_tab", "s_tOT", "s_vOT", "s_s1h", "s_wh", "s_wr", "s_yz", "s_ga", "s_ga1", "s_gw2", "s_gw3", "s_sc", "s_sc1", "s_rep", "s_vLst", "s_gLst", "s_yb0", "s_yb1", "s_tB16",
             "s_cE", "s_cDen", "s_cRB", "s_cSilu", "s_cStd", "s_cCast", "s_gw"]
    S = {n: Sem(n) for n in names}

    sb = lambda st, name, shape, dt: st.enter_context(nc.sbuf_tensor(name, shape, dt))
    master = sb(es, "master", [128, c.TC * c.D], F32)
    ring4 = [sb(es, f"r4_{i}", [128, c.D], BF16) for i in range(5)]
    xtm = [sb(es, f"xtm_{i}", [128, c.D], BF16) for i in range(3)]
    maskT = sb(es, "maskT_s", [128, c.R * 128], F32)
    identb = sb(es, "identb_s", [128, 128], BF16)
    ones1 = sb(es, "ones1_s", [1, 128], F32)
    vcol = sb(es, "vcol_s", [128, c.NB], F32)
    S_acc = sb(es, "S_acc", [128, c.G * 129], F32)
    S_bf = sb(es, "S_bf", [128, c.G * 129], BF16)
    w_all = sb(es, "w_all", [128, c.TC * c.NE], F32)
    gwm = sb(es, "gwm_s", [128, c.DC * c.NE], BF16)
    red1 = sb(es, "red1", [128, 1], F32)
    red2 = sb(es, "red2", [128, 1], F32)
    rstd = sb(es, "rstd", [128, 1], F32)
    sqscr = sb(es, "sqscr", [128, 512], F32)
    bnbuf = sb(es, "bnbuf", [128, 6 * (c.D // 512)], F32)
    bnmv = sb(es, "bnmv", [128, 2], F32)
    g_sc = sb(es, "g_sc", [128, c.NE], F32)
    g_t = sb(es, "g_t", [128, c.NE], F32)
    g_m = sb(es, "g_m", [128, 1], F32)
    d_col = sb(es, "d_col", [128, 2], F32)
    d_col2 = sb(es, "d_col2", [128, 2], F32)

    nsync, ngp, nt, nv, nsc = nc.sync, nc.gpsimd, nc.tensor, nc.vector, nc.scalar

    P4b = P[4][:, :].bitcast(BF16)
    P56b = [P[5][:, :].bitcast(BF16), P[6][:, :].bitcast(BF16)]
    Pnb = [P[i][:, :].bitcast(BF16) for i in range(1, 5)]

    def tp_dst(i):
        return PB[:, 0:128] if i % 2 == 0 else P4b[:, 0:128]

    w4sem = [S["s_w4a"], S["s_w4b"], S["s_w4c"], S["s_w4d"], S["s_w4e"]]
    r4_hist = []
    consumed4 = []
    w4_idx = {"n": 0}

    def dma_ring4(src_ap):
        i = w4_idx["n"]
        w4_idx["n"] += 1
        if i >= 5:
            sem_h, val = consumed4[i - 5]
            ngp.wait_ge(sem_h, val)
        ins = ngp.dma_start(ring4[i % 5][:, 0:src_ap.shape[1]], src_ap)
        w4sem[i % 5].inc(ins, 16)
        r4_hist.append(w4sem[i % 5].n)
        return i

    def use_ring4(i):
        nt.wait_ge(w4sem[i % 5].h, r4_hist[i])
        return ring4[i % 5]

    w2sem = [S["s_w2a"], S["s_w2b"]]
    r2 = {"ring": None, "n": 0, "hist": []}
    consumed2 = []

    def dma_ring2(src_ap):
        i = r2["n"]
        r2["n"] += 1
        if i >= 2:
            sem_h, val = consumed2[i - 2]
            ngp.wait_ge(sem_h, val)
        ins = ngp.dma_start(r2["ring"][i % 2][:, 0:src_ap.shape[1]], src_ap)
        w2sem[i % 2].inc(ins, 16)
        r2["hist"].append(w2sem[i % 2].n)
        return i

    def use_ring2(i):
        nt.wait_ge(w2sem[i % 2].h, r2["hist"][i])
        return r2["ring"][i % 2]

    att = contextlib.ExitStack()
    att2 = contextlib.ExitStack()
    aoutT = att.enter_context(
        nc.sbuf_tensor("aoutT", [128, c.HQ * c.T], BF16, side="right"))
    ropeq = [sb(att2, f"ropeq{i}", [128, 512], F32) for i in range(2)]
    ropes = [sb(att2, f"ropes{i}", [128, 512], F32) for i in range(2)]
    tmp1 = [sb(att2, f"tmp1_{i}", [128, 512], F32) for i in range(2)]
    tmp2 = sb(att2, "tmp2", [128, 512], F32)
    linb = sb(att2, "linb", [128, 512], F32)
    eexb = sb(att2, "eexb", [128, 512], F32)
    A_sb = [sb(att2, f"A_sb{i}", [128, c.R * 128], BF16) for i in range(2)]
    KfT_sb = [sb(att2, f"KfT{i}", [128, 128], BF16) for i in range(2)]
    Vf_sb = sb(att2, "Vf_sb", [128, 512], BF16)
    ot_tm = [sb(att2, f"ot_tm{i}", [128, c.R * 128], BF16) for i in range(2)]

    rope_j = {"n": 0}
    rope_hist = {}
    rope_pend = []

    def rope_s1(ps, ctab, stab, tab_col, dst_ap, w):
        j = rope_j["n"]
        rope_j["n"] += 1
        par = j % 2
        q_sb, q_sh = ropeq[par], ropes[par]
        h = {}
        S["s_shift"].wait(nv)
        nv.drain()
        nv.tensor_copy(q_sb[:, 0:w], ps[:, 0:w])
        nv.drain()
        ins = nv.tensor_tensor(tmp1[par][:, 0:w], q_sb[:, 0:w],
                               ctab[:, tab_col:tab_col + w], ALU.mult)
        S["s_vQs"].inc(ins, 1)
        h["qs"] = S["s_vQs"].n
        if j >= 2:
            nsc.wait_ge(S["s_vRope"].h, rope_hist[j - 2]["qr"])
        nsc.wait_ge(S["s_vQs"].h, h["qs"])
        ins = nsc.dma_start(q_sh[0:64, 0:w], q_sb[64:128, 0:w])
        S["s_shift"].inc(ins, 16)
        ins = nsc.dma_start(q_sh[64:128, 0:w], q_sb[0:64, 0:w])
        S["s_shift"].inc(ins, 16)
        h["sh"] = S["s_shift"].n
        nv.wait_ge(S["s_shift"].h, h["sh"])
        nv.drain()
        nv.tensor_tensor(tmp2[:, 0:w], q_sh[:, 0:w],
                         stab[:, tab_col:tab_col + w], ALU.mult)
        nv.drain()
        ins = nv.tensor_tensor(tmp1[par][:, 0:w], tmp1[par][:, 0:w],
                               tmp2[:, 0:w], ALU.add)
        S["s_vRope"].inc(ins, 1)
        h["qr"] = S["s_vRope"].n
        rope_hist[j] = h
        rope_pend.append((j, dst_ap, w))
        return j

    def rope_s2(j):
        par = j % 2
        w = rope_pend_w(j)
        nsc.wait_ge(S["s_vRope"].h, rope_hist[j]["qr"])
        if j >= 1 and "s3" in rope_hist.get(j - 1, {}):
            nsc.wait_ge(S["s_vRope"].h, rope_hist[j - 1]["s3"])
        nsc.drain()
        nsc.activation(linb[:, 0:512], tmp1[par][:, 0:512], ACTF.Relu)
        nsc.drain()
        nsc.activation(eexb[:, 0:512], tmp1[par][:, 0:512], ACTF.Relu,
                       scale=-1.0)
        nsc.drain()
        ins = nsc.activation(eexb[:, 0:512], eexb[:, 0:512], ACTF.Exp,
                             scale=-1.0)
        S["s_cE"].inc(ins, 1)
        rope_hist[j]["ce"] = S["s_cE"].n

    def rope_pend_w(j):
        return 512

    def rope_s3(j, dst_ap, w):
        nv.wait_ge(S["s_cE"].h, rope_hist[j]["ce"])
        nv.drain()
        ins = nv.tensor_tensor(dst_ap, eexb[:, 0:w], linb[:, 0:w], ALU.add)
        S["s_vRope"].inc(ins, 1)
        rope_hist[j]["s3"] = S["s_vRope"].n

    def rope_drain(upto=0):
        while len(rope_pend) > upto:
            j, dst_ap, w = rope_pend.pop(0)
            rope_s2(j)
            rope_s3(j, dst_ap, w)

    def rope_step():
        rope_drain(1)

    glob_u = {"n": 0}
    scan_u = {"n": 0}
    snap_g = {}
    rec = {}
    khist = {}
    tsu = {}
    sjh = {}

    for b in range(c.NB):
        is_pre = b < c.NBP
        blk = contextlib.ExitStack()
        xT_b = sb(blk, f"xT_{b}", [128, c.DC * 512], BF16)
        ctab_b = sb(blk, f"ctab_{b}", [128, 512], F32)
        stab_b = sb(blk, f"stab_{b}", [128, 512], F32)
        Kf_b = sb(blk, f"Kf_{b}", [128, c.G * 512], BF16)
        V_b = sb(blk, f"V_{b}", [128, c.BTC * c.G * 129], BF16)
        Qf_b = None if is_pre else sb(blk, f"Qf_{b}", [128, c.HQ * 512], BF16)
        V_b3 = V_b[:, :].rearrange("p (t g e) -> p t g e", t=c.BTC, g=c.G)

        with nc.Block() as _blk:
            @_blk.sync
            def _(_, b=b, is_pre=is_pre, xT_b=xT_b, Kf_b=Kf_b, V_b=V_b,
                  Qf_b=Qf_b, V_b3=V_b3, ctab_b=ctab_b, stab_b=stab_b):
                if b == 0:
                    for dst, src in [(maskT, d_mask), (identb, d_id),
                                     (ones1, d_on1), (vcol, d_vcol)]:
                        ins = nsync.dma_start(dst[:, :], src)
                        S["s_cst"].inc(ins, 16)
                    for tc in range(c.TC):
                        ins = nsync.dma_start(master[:, tc * c.D:(tc + 1) * c.D],
                                              d_x[tc * 128:(tc + 1) * 128, :])
                        S["s_xm"].inc(ins, 16)
                    ins = ngp.dma_start(gwm[:, :], d_gw)
                    S["s_gw"].inc(ins, 16)
                    ins = nv.memset(S_acc[:, :], 0.0)
                    S["s_vSnap"].inc(ins, 1)
                    S["s_cst"].wait(nv)
                    S["s_cst"].wait(nt)

                ins = nsync.dma_start(ctab_b[:, :], d_cos[:, b * 512:(b + 1) * 512])
                S["s_tab"].inc(ins, 16)
                ins = nsync.dma_start(stab_b[:, :], d_sin[:, b * 512:(b + 1) * 512])
                S["s_tab"].inc(ins, 16)
                S["s_tab"].wait(nv)

                xsrc = d_xp if is_pre else d_x
                row0 = (b if is_pre else b - c.NBP) * 512
                xtmsems = [S["s_xtm0"], S["s_xtm1"], S["s_xtm2"]]
                for tcb in range(c.BTC):
                    k = b * c.BTC + tcb
                    par = k % 3
                    if k >= 3:
                        ngp.wait_ge(S["s_tTx"].h, (k - 2) * c.DC)
                    ins = ngp.dma_start(
                        xtm[par][:, :],
                        xsrc[row0 + tcb * 128: row0 + (tcb + 1) * 128, :])
                    xtmsems[par].inc(ins, 16)
                    xtmsems[par].wait(nt)
                    for dc in range(c.DC):
                        tix = k * c.DC + dc
                        if tix >= 2:
                            nt.wait_ge(S["s_vXT"].h, tix - 1)
                        dst = tp_dst(tix)
                        ins = nt.transpose(dst,
                                           xtm[par][:, dc * 128:(dc + 1) * 128],
                                           identb[:, :])
                        S["s_tTx"].inc(ins, 1)
                        S["s_tTx"].wait(nv)
                        nv.drain()
                        ins = nv.tensor_copy(
                            xT_b[:, dc * 512 + tcb * 128: dc * 512 + (tcb + 1) * 128],
                            dst)
                        S["s_vXT"].inc(ins, 1)

                proj_units = [("k", g) for g in range(c.G)] + \
                             [("v", g) for g in range(c.G)] + \
                             ([] if is_pre else [("q", h) for h in range(c.HQ)])

                proj_free = []
                S["s_vXT"].wait(nt)
                projP = [P[0], P[1], P[2], P[3], P[5], P[6]]
                for iu, (kind, idx) in enumerate(proj_units):
                    src_d = {"k": d_wk, "v": d_wv, "q": d_wq}[kind]
                    si = dma_ring4(src_d[idx])
                    bank = projP[iu % 6]
                    if iu >= 6:
                        sem, val = proj_free[iu - 6]
                        nt.wait_ge(sem.h, val)
                    slab = use_ring4(si)
                    for dc in range(c.DC):
                        ins = nt.matmul(bank[:, :],
                                        slab[:, dc * 128:(dc + 1) * 128],
                                        xT_b[:, dc * 512:(dc + 1) * 512],
                                        start=(dc == 0), stop=(dc == c.DC - 1))
                    S["s_tP"].inc(ins, 1)
                    consumed4.append((S["s_tP"].h, S["s_tP"].n))
                    if kind == "v":
                        S["s_tP"].wait(nv)
                        nv.drain()
                        ins = nv.tensor_copy(Vf_sb[:, :], bank[:, :])
                        S["s_vVf"].inc(ins, 1)
                        proj_free.append((S["s_vVf"], S["s_vVf"].n))
                        S["s_vVf"].wait(nt)
                        for tcb in range(c.BTC):
                            vtix = S["s_tTv"].n
                            if vtix >= 2:
                                nt.wait_ge(S["s_vVtm"].h, S["s_vVtm"].n - 1)
                            dst = tp_dst(vtix)
                            ins = nt.transpose(dst,
                                               Vf_sb[:, tcb * 128:(tcb + 1) * 128],
                                               identb[:, :])
                            S["s_tTv"].inc(ins, 1)
                            S["s_tTv"].wait(nv)
                            nv.drain()
                            ins = nv.tensor_copy(V_b3[:, tcb, idx, 0:128], dst)
                            S["s_vVtm"].inc(ins, 1)
                    else:
                        S["s_tP"].wait(nv)
                        dst = (Kf_b if kind == "k" else Qf_b)[
                            :, idx * 512:(idx + 1) * 512]
                        rope_s1(bank, ctab_b, stab_b, 0, dst, 512)
                        proj_free.append((S["s_vQs"], S["s_vQs"].n))
                        rope_step()

                for tcb in range(c.BTC):
                    nv.drain()
                    ins = nv.tensor_copy(
                        V_b3[:, tcb, :, 128:129],
                        vcol[:, b:b + 1].broadcast_to((128, c.G, 1)))
                    S["s_vVtm"].inc(ins, 1)

                rope_drain(0)
                S["s_vRope"].wait(nt)
                S["s_vVtm"].wait(nt)

                Qf3 = None if is_pre else Qf_b[:, :].rearrange(
                    "p (h t) -> p h t", h=c.HQ)
                W = c.R * 128

                def sA(su, g, tcb):
                    nt.wait_ge(S["s_vAm"].h, su)
                    qmove = Qf3[:, g * c.R:(g + 1) * c.R,
                                tcb * 128:(tcb + 1) * 128]
                    ins = nt.matmul(P[0][:, 0:W],
                                    Kf_b[:, g * 512 + tcb * 128:
                                         g * 512 + (tcb + 1) * 128],
                                    qmove, start=True, stop=True)
                    S["s_tA"].inc(ins, 1)
                    rec[su] = {"tah": S["s_tA"].n}

                def sB(su, g, tcb):
                    nv.wait_ge(S["s_tA"].h, rec[su]["tah"])
                    if su - 2 in rec and "tnh" in rec[su - 2]:
                        nv.wait_ge(S["s_tN"].h, rec[su - 2]["tnh"])
                    nv.drain()
                    ins = nv.tensor_tensor(A_sb[su % 2][:, 0:W], P[0][:, 0:W],
                                           maskT[:, 0:W], ALU.mult)
                    S["s_vAm"].inc(ins, 1)
                    rec[su]["amh"] = S["s_vAm"].n

                def sC(su, g, tcb):
                    nt.wait_ge(S["s_vAm"].h, rec[su]["amh"])
                    nt.wait_ge(S["s_vSnap"].h, snap_g.get(g, 1))
                    if su - 2 in rec and "evh" in rec[su - 2]:
                        nt.wait_ge(S["s_vOT"].h, rec[su - 2]["evh"])
                    pair = (P[1], P[2]) if su % 2 == 0 else (P[3], P[4])
                    for r in range(c.R):
                        bank = pair[r // 2]
                        off = (r % 2) * 129
                        h = g * c.R + r
                        nt.matmul(bank[:, off:off + 129],
                                  A_sb[su % 2][:, r * 128:(r + 1) * 128],
                                  V_b3[:, tcb, g, 0:129],
                                  start=True, stop=False)
                        ins = nt.matmul(bank[:, off:off + 129],
                                        Qf3[:, h, tcb * 128:(tcb + 1) * 128],
                                        S_bf[:, g * 129:(g + 1) * 129],
                                        start=False, stop=True)
                    S["s_tN"].inc(ins, 1)
                    rec[su]["tnh"] = S["s_tN"].n

                def sD(su, g, tcb):
                    nv.wait_ge(S["s_tN"].h, rec[su]["tnh"])
                    if su - 2 in rec and "oth" in rec[su - 2]:
                        nv.wait_ge(S["s_tOT"].h, rec[su - 2]["oth"])
                    pair = (P[1], P[2]) if su % 2 == 0 else (P[3], P[4])
                    for pi in range(max(1, c.R // 2)):
                        bank = pair[pi]
                        nh = min(2, c.R)
                        den = bank[:, 0:nh * 129].rearrange(
                            "p (r e) -> p r e", e=129)[:, :, 128]
                        nv.drain()
                        nv.tensor_scalar(d_col[:, 0:nh], den, EPS_AT, None,
                                         ALU.add)
                        nv.drain()
                        nv.reciprocal(d_col2[:, 0:nh], d_col[:, 0:nh])
                        nv.drain()
                        for q in range(nh):
                            r = pi * 2 + q
                            ins = nv.tensor_scalar(
                                ot_tm[su % 2][:, r * 128:(r + 1) * 128],
                                bank[:, q * 129:q * 129 + 128],
                                d_col2[:, q:q + 1], None, ALU.mult)
                            S["s_vDiv"].inc(ins, 1)
                    rec[su]["dvh"] = S["s_vDiv"].n

                def sEF(su, g, tcb):
                    nt.wait_ge(S["s_vDiv"].h, rec[su]["dvh"])
                    if su - 2 in rec and "evh" in rec[su - 2]:
                        nt.wait_ge(S["s_vOT"].h, rec[su - 2]["evh"])
                    t0 = (b - c.NBP) * 512 + tcb * 128
                    for r in range(c.R):
                        pb = Pnb[((su % 2) * 2) + (r % 2)]
                        treg = pb[:, 516 + (r // 2) * 128:
                                  516 + (r // 2) * 128 + 128]
                        if r >= 2:
                            nt.wait_ge(S["s_vOT"].h, S["s_vOT"].n - 1)
                        ins = nt.transpose(
                            treg, ot_tm[su % 2][:, r * 128:(r + 1) * 128],
                            identb[:, :])
                        S["s_tOT"].inc(ins, 1)
                        nsc.wait_ge(S["s_tOT"].h, S["s_tOT"].n)
                        nsc.drain()
                        h = g * c.R + r
                        ins = nsc.activation(
                            aoutT[:, h * c.T + t0: h * c.T + t0 + 128], treg,
                            ACTF.Copy)
                        S["s_vOT"].inc(ins, 1)
                    rec[su]["oth"] = S["s_tOT"].n
                    rec[su]["evh"] = S["s_vOT"].n

                def stA(u, g, tcb):
                    parU = u % 2
                    tdst = P56b[parU][:, 512:640]
                    if u - 2 in sjh:
                        nt.wait_ge(S["s_vSnap"].h, sjh[u - 2])
                    ins = nt.transpose(tdst,
                                       Kf_b[:, g * 512 + tcb * 128:
                                            g * 512 + (tcb + 1) * 128],
                                       identb[:, :])
                    S["s_tS"].inc(ins, 1)
                    nv.wait_ge(S["s_tS"].h, S["s_tS"].n)
                    if u - 2 in tsu:
                        nv.wait_ge(S["s_tS"].h, tsu[u - 2])
                    nv.drain()
                    ins = nv.tensor_copy(KfT_sb[parU][:, :], tdst)
                    S["s_vSnap"].inc(ins, 1)
                    khist[u] = S["s_vSnap"].n

                def stB(u, g, tcb):
                    parU = u % 2
                    nt.wait_ge(S["s_vSnap"].h, khist[u])
                    ins = nt.matmul(P[5 + parU][:, 0:129],
                                    KfT_sb[parU][:, :],
                                    V_b3[:, tcb, g, 0:129],
                                    start=True, stop=True)
                    S["s_tS"].inc(ins, 1)
                    tsu[u] = S["s_tS"].n
                    nv.wait_ge(S["s_tS"].h, tsu[u])
                    nv.drain()
                    nv.tensor_tensor(S_acc[:, g * 129:(g + 1) * 129],
                                     S_acc[:, g * 129:(g + 1) * 129],
                                     P[5 + parU][:, 0:129], ALU.add)
                    nv.drain()
                    ins = nv.tensor_copy(S_bf[:, g * 129:(g + 1) * 129],
                                         S_acc[:, g * 129:(g + 1) * 129])
                    S["s_vSnap"].inc(ins, 1)
                    sjh[u] = S["s_vSnap"].n
                    snap_g[g] = S["s_vSnap"].n

                units = []
                for tcb in range(c.BTC):
                    for g in range(c.G):
                        u = glob_u["n"]
                        glob_u["n"] += 1
                        if not is_pre:
                            su = scan_u["n"]
                            scan_u["n"] += 1
                        else:
                            su = None
                        units.append((su, g, tcb, u))

                if is_pre:
                    for i in range(len(units) + 1):
                        if i < len(units):
                            _, g, tcb, u = units[i]
                            stA(u, g, tcb)
                        if i >= 1:
                            _, g, tcb, u = units[i - 1]
                            stB(u, g, tcb)
                else:
                    n_u = len(units)
                    for i in range(n_u + 3):
                        if i < n_u:
                            su, g, tcb, u = units[i]
                            sA(su, g, tcb)
                        if 1 <= i < n_u + 1:
                            su, g, tcb, u = units[i - 1]
                            sB(su, g, tcb)
                            sC(su, g, tcb)
                        if 2 <= i < n_u + 2:
                            su, g, tcb, u = units[i - 2]
                            sD(su, g, tcb)
                            sEF(su, g, tcb)
                            stA(u, g, tcb)
                            stB(u, g, tcb)

                if b == c.NB - 1:
                    if "qf" in d_dbg and Qf_b is not None:
                        S["s_vRope"].wait(nsync)
                        ins = nsync.dma_start(d_dbg["qf"], Qf_b[:, :])
                        S["s_dbg"].inc(ins, 16)
                        S["s_dbg"].wait(nsync)
                    if "kf" in d_dbg:
                        S["s_vRope"].wait(nsync)
                        ins = nsync.dma_start(d_dbg["kf"], Kf_b[:, :])
                        S["s_dbg"].inc(ins, 16)
                        S["s_dbg"].wait(nsync)
                    if "vb" in d_dbg:
                        S["s_vVtm"].wait(nsync)
                        ins = nsync.dma_start(d_dbg["vb"], V_b[:, :])
                        S["s_dbg"].inc(ins, 16)
                        S["s_dbg"].wait(nsync)
                    if "sacc" in d_dbg:
                        S["s_vSnap"].wait(nsync)
                        ins = nsync.dma_start(d_dbg["sacc"], S_acc[:, :])
                        S["s_dbg"].inc(ins, 16)
                        S["s_dbg"].wait(nsync)

        blk.close()

    if "aout" in d_dbg:
        with nc.Block() as _b:
            @_b.sync
            def _(_):
                S["s_vDiv"].wait(nsync)
                S["s_vOT"].wait(nsync)
                ins = nsync.dma_start(d_dbg["aout"], aoutT[:, :])
                S["s_dbg"].inc(ins, 16)
                S["s_dbg"].wait(nsync)

    att2.close()

    wob = contextlib.ExitStack()
    r2w = max(c.ICN, min(c.HQ, 8)) * 512
    r2["ring"] = [sb(wob, f"r2_{i}", [128, r2w], BF16) for i in range(2)]
    mirror = sb(wob, "mirror", [128, c.DC * c.T], BF16)

    with nc.Block() as _blk:
        @_blk.sync
        def _(_):
            S["s_vOT"].wait(nt)
            S["s_xm"].wait(nv)

            def emit_mirror_tp(tc):
                S["s_cCast"].wait(nt)
                for dc in range(c.DC):
                    tix = tc * c.DC + dc
                    if tix >= 2:
                        nt.wait_ge(S["s_vMir"].h, tix - 1)
                    dst = tp_dst(tix)
                    ins = nt.transpose(dst,
                                       xtm[tc % 3][:, dc * 128:(dc + 1) * 128],
                                       identb[:, :])
                    S["s_tTm"].inc(ins, 1)
                    S["s_tTm"].wait(nv)
                    nv.drain()
                    ins = nv.tensor_copy(
                        mirror[:, dc * c.T + tc * 128: dc * c.T + (tc + 1) * 128],
                        dst)
                    S["s_vMir"].inc(ins, 1)

            for oi in range(c.OI):
                h8 = min(c.HQ, 8)
                nhalf = c.HQ // h8
                sis = [dma_ring2(d_wo[oi][:, hh * h8 * 512:(hh + 1) * h8 * 512])
                       for hh in range(nhalf)]
                slabs = [use_ring2(s) for s in sis]
                for tc in range(c.TC):
                    bank = P[tc % 4]
                    if oi * c.TC + tc >= 4:
                        nt.wait_ge(S["s_vWo"].h, S["s_vWo"].n - 3)
                    for fc in range(c.HQ):
                        ins = nt.matmul(
                            bank[:, :],
                            aoutT[:, fc * c.T + tc * 128: fc * c.T + (tc + 1) * 128],
                            slabs[fc // h8][:, (fc % h8) * 512:(fc % h8 + 1) * 512],
                            start=(fc == 0), stop=(fc == c.HQ - 1))
                    S["s_tWO"].inc(ins, 1)
                    if tc == c.TC - 1:
                        for _hh in range(nhalf):
                            consumed2.append((S["s_tWO"].h, S["s_tWO"].n))
                    S["s_tWO"].wait(nv)
                    nv.drain()
                    msl = master[:, tc * c.D + oi * 512: tc * c.D + (oi + 1) * 512]
                    ins = nv.tensor_tensor(msl, msl, bank[:, :], ALU.add)
                    S["s_vWo"].inc(ins, 1)
                    if oi == c.OI - 1:
                        msl = master[:, tc * c.D:(tc + 1) * c.D]
                        nv.drain()
                        for gi in range(c.D // 512):
                            nv.bn_stats(bnbuf[:, gi * 6:(gi + 1) * 6],
                                        msl[:, gi * 512:(gi + 1) * 512])
                            nv.drain()
                        nv.bn_aggr(bnmv[:, :], bnbuf[:, :])
                        nv.drain()
                        nv.tensor_scalar(red2[:, :], bnmv[:, 1:2], 1.0, EPS_LN,
                                         ALU.mult, ALU.add)
                        nv.drain()
                        ins = nv.reciprocal(rstd[:, :], red2[:, :])
                        S["s_vLN"].inc(ins, 1)
                        S["s_vLN"].wait(nsc)
                        nsc.drain()
                        ins = nsc.activation(rstd[:, :], rstd[:, :], ACTF.Sqrt)
                        S["s_cStd"].inc(ins, 1)
                        S["s_cStd"].wait(nv)
                        nv.drain()
                        ins = nv.tensor_scalar(msl, msl, bnmv[:, 0:1], rstd[:, :],
                                               ALU.subtract, ALU.mult)
                        S["s_vLN"].inc(ins, 1)
                        S["s_vLN"].wait(nsc)
                        nsc.drain()
                        if tc >= 3:
                            nsc.wait_ge(S["s_tTm"].h, (tc - 2) * c.DC)
                            nsc.wait_ge(S["s_s1h"].h, 16 * (tc - 2))
                        ins = nsc.copy(xtm[tc % 3][:, :], msl)
                        S["s_cCast"].inc(ins, 1)
                        S["s_cCast"].wait(nsync)
                        S["s_s1h"].wait(nsync)
                        ins = nsync.dma_start(
                            d_s1h[tc * 128:(tc + 1) * 128, :], xtm[tc % 3][:, :])
                        S["s_s1h"].inc(ins, 16)
                        if tc >= 1:
                            emit_mirror_tp(tc - 1)
            emit_mirror_tp(c.TC - 1)

    if "r1s" in d_dbg:
        with nc.Block() as _b:
            @_b.sync
            def _(_):
                S["s_vLN"].wait(nsync)
                ins = nsync.dma_start(d_dbg["r1s"], master[:, :])
                S["s_dbg"].inc(ins, 16)
                S["s_dbg"].wait(nsync)
    if "mir" in d_dbg:
        with nc.Block() as _b:
            @_b.sync
            def _(_):
                S["s_vMir"].wait(nsync)
                ins = nsync.dma_start(d_dbg["mir"], mirror[:, :])
                S["s_dbg"].inc(ins, 16)
                S["s_dbg"].wait(nsync)

    att.close()

    ffb = contextlib.ExitStack()
    hp = sb(ffb, "hp", [128, c.ICN * 512], BF16)
    t1b = [sb(ffb, f"t1_{i}", [128, 512], F32) for i in range(2)]
    gx = [sb(ffb, f"gx_{i}", [128, c.DC * c.CAP], BF16) for i in range(2)]
    wg = [sb(ffb, f"wg_{i}", [128, (c.CAP // 128) * 64], F32) for i in range(2)]
    _nsc = 2 if c.OI >= 2 else 1
    _wsc = (c.OI // _nsc) * 512
    ysbq = [sb(ffb, f"ysb_{i}", [128, (c.CAP // 128) * _wsc], BF16)
            for i in range(2)]
    _TW = c.T // 16
    _CWE = c.NE * (c.CAP // 16)
    gxf0 = gx[0][0:16, :].bitcast(F32)
    gxf1 = gx[1][0:16, :].bitcast(F32)
    assert 2 * c.NE * _TW <= gxf0.shape[1]
    s_wwe = gxf0[:, 0:c.NE * _TW]
    s_arr = gxf0[:, c.NE * _TW:2 * c.NE * _TW]
    s_cmp = gxf1[:, 0:c.NE * _TW]
    s_tok = gxf1[:, c.NE * _TW:2 * c.NE * _TW]
    s_msk = gxf0[:, c.NE * _TW:c.NE * _TW + _CWE]
    s_san = gxf0[:, 0:_CWE]
    s_iotc = sb(ffb, "s_iotc", [16, _CWE], F32)
    s_on16 = sb(ffb, "s_on16", [1, 16], F32)
    s_nfa = sb(ffb, "s_nfa", [1, c.NE], mybir.dt.uint32)
    s_nff = sb(ffb, "s_nff", [1, c.NE], F32)
    s_nfi = sb(ffb, "s_nfi", [1, c.NE], mybir.dt.int32)
    s_nf16 = sb(ffb, "s_nf16", [16, c.NE], F32)
    s_idxa = sb(ffb, "s_idxa", [128, c.NE * (c.CAP // 16)], mybir.dt.int16)
    rnum = ffb.enter_context(nc.gpsimd.register("rnum"))
    rnum2 = ffb.enter_context(nc.gpsimd.register("rnum2"))
    rnum3 = ffb.enter_context(nc.gpsimd.register("rnum3"))
    rg2 = [rnum, rnum2, rnum3]
    from concourse import library_config as _LC

    CW = c.CAP // 16
    NCH = c.CAP // 128
    N_SC = _nsc
    OSC = c.OI // N_SC
    W_SC = OSC * 512

    with nc.Block() as _blk:
        @_blk.sync
        def _(_):
            ins = ngp.memset(gx[0][:, :], 0.0)
            ins = ngp.memset(gx[1][:, :], 0.0)
            ins = ngp.memset(wg[0][:, :], 0.0)
            ins = ngp.memset(wg[1][:, :], 0.0)
            ngp.drain()
            S["s_gLst"].inc(ins, 1)
            gx_init = S["s_gLst"].n
            ins = nv.memset(sqscr[:, :], 0.0)
            S["s_vLst"].inc(ins, 1)
            S["s_vLst"].wait(nsync)
            sqb = sqscr[:, :].bitcast(BF16)
            zw = min(sqb.shape[1], c.D)
            sqb = sqb[:, 0:zw]
            for tc in range(c.TC):
                for oz in range(c.D // zw):
                    ins = nsync.dma_start(
                        d_yh[tc * 128:(tc + 1) * 128, oz * zw:(oz + 1) * zw],
                        sqb)
                    S["s_yz"].inc(ins, 16)
            nsc.wait_ge(S["s_gLst"].h, gx_init)
            ins = nsc.dma_start(s_tok[:, :], d_tokp1)
            S["s_wr"].inc(ins, 16)
            ins = nsc.dma_start(s_iotc[:, :], d_iotac)
            S["s_wr"].inc(ins, 16)
            ins = nsc.dma_start(s_on16[:, :], d_on16)
            S["s_wr"].inc(ins, 16)

            S["s_vMir"].wait(nt)
            S["s_gw"].wait(nt)
            for tc in range(c.TC):
                for dc in range(c.DC):
                    ins = nt.matmul(P[4][:, tc * c.NE:(tc + 1) * c.NE],
                                    mirror[:, dc * c.T + tc * 128:
                                           dc * c.T + (tc + 1) * 128],
                                    gwm[:, dc * c.NE:(dc + 1) * c.NE],
                                    start=(dc == 0), stop=(dc == c.DC - 1))
            S["s_tG"].inc(ins, 1)
            for tc in range(c.TC):
                bank = P[4]
                S["s_tG"].wait(nv)
                nv.drain()
                nv.tensor_reduce(g_m[:, :], bank[:, tc * c.NE:(tc + 1) * c.NE],
                                 AX.X, ALU.max)
                nv.drain()
                ins = nv.tensor_scalar(g_sc[:, :],
                                       bank[:, tc * c.NE:(tc + 1) * c.NE],
                                       g_m[:, :], None, ALU.subtract)
                S["s_vGate"].inc(ins, 1)
                S["s_vGate"].wait(nsc)
                nsc.drain()
                ins = nsc.activation(g_sc[:, :], g_sc[:, :], ACTF.Exp)
                S["s_cSilu"].inc(ins, 1)
                S["s_cSilu"].wait(nv)
                nv.drain()
                nv.tensor_reduce(g_m[:, :], g_sc[:, :], AX.X, ALU.add)
                nv.drain()
                nv.reciprocal(g_m[:, :], g_m[:, :])
                nv.drain()
                nv.tensor_scalar(g_sc[:, :], g_sc[:, :], g_m[:, :], None, ALU.mult)
                nv.drain()
                nv.tensor_reduce(g_m[:, :], g_sc[:, :], AX.X, ALU.max)
                nv.drain()
                nv.tensor_scalar(g_t[:, :], g_sc[:, :], g_m[:, :], None, ALU.is_ge)
                nv.drain()
                nv.scalar_tensor_tensor(g_t[:, :], g_t[:, :], -1e30, g_sc[:, :],
                                        ALU.mult, ALU.add)
                nv.drain()
                nv.tensor_reduce(g_m[:, :], g_t[:, :], AX.X, ALU.max)
                nv.drain()
                nv.tensor_scalar(g_t[:, :], g_sc[:, :], g_m[:, :], None, ALU.is_ge)
                nv.drain()
                ins = nv.tensor_tensor(w_all[:, tc * c.NE:(tc + 1) * c.NE],
                                       g_sc[:, :], g_t[:, :], ALU.mult)
                S["s_vGate"].inc(ins, 1)
                S["s_vGate"].wait(nsync)
                ins = nsync.dma_start(d_wh[tc * 128:(tc + 1) * 128, 0:c.NE],
                                      w_all[:, tc * c.NE:(tc + 1) * c.NE])
                S["s_wh"].inc(ins, 16)

            TW = c.T // 16
            S["s_vGate"].wait(nsync)
            nsync.wait_ge(S["s_gLst"].h, gx_init)
            for e in range(c.NE):
                for jj in range(8):
                    srcw = w_all[jj * 16:(jj + 1) * 16, :].rearrange(
                        "p (t e) -> p t e", e=c.NE)[:, :, e]
                    ins = nsync.dma_start(
                        s_wwe[:, e * TW + jj * c.TC:e * TW + (jj + 1) * c.TC],
                        srcw, allow_slow_non_contiguous=True)
                    S["s_wr"].inc(ins, 16)
            resh_wm = S["s_wr"].n
            lists_state = {}

            def hook_arr():
                nv.wait_ge(S["s_wr"].h, resh_wm)
                nv.drain()
                nv.tensor_scalar(s_arr[:, :], s_wwe[:, :], 0.0, None,
                                 ALU.is_gt)
                nv.drain()
                nv.tensor_tensor(s_arr[:, :], s_arr[:, :], s_tok[:, :],
                                 ALU.mult)
                nv.drain()
                ins = nv.tensor_scalar(s_arr[:, :], s_arr[:, :], -1.0, None,
                                       ALU.add)
                S["s_vLst"].inc(ins, 1)
                lists_state["arrv"] = S["s_vLst"].n

            def hook_sg():
                ngp.wait_ge(S["s_vLst"].h, lists_state["arrv"])
                nc.gpsimd.load_library(_LC.sparse_gather)
                for e in range(c.NE):
                    ins = ngp.sparse_gather(
                        s_cmp[:, e * TW:(e + 1) * TW],
                        s_arr[:, e * TW:(e + 1) * TW],
                        num_found=s_nfa[0:1, e:e + 1])
                S["s_gLst"].inc(ins, 1)
                nc.gpsimd.load_library(_LC.mlp)

            def hook_post():
                S["s_gLst"].wait(nv)
                nv.drain()
                nv.tensor_copy(s_nff[:, :], s_nfa[:, :])
                nv.drain()
                ins = nv.tensor_scalar(s_nff[:, :], s_nff[:, :],
                                       float(c.CAP), None, ALU.min)
                S["s_vLst"].inc(ins, 1)
                vmin = S["s_vLst"].n
                nt.wait_ge(S["s_vLst"].h, vmin)
                ins = nt.matmul(P[4][0:16, 0:c.NE], s_on16[:, :],
                                s_nff[0:1, :], start=True, stop=True)
                S["s_tB16"].inc(ins, 1)
                S["s_tB16"].wait(nv)
                nv.drain()
                nv.tensor_copy(s_nf16[:, :], P[4][0:16, 0:c.NE])
                nv.drain()
                for e in range(c.NE):
                    nv.tensor_scalar(s_msk[:, e * CW:(e + 1) * CW],
                                     s_iotc[:, e * CW:(e + 1) * CW],
                                     s_nf16[:, e:e + 1], None, ALU.is_lt)
                nv.drain()
                cmp3 = s_cmp[:, :].rearrange("p (e w) -> p e w",
                                             w=TW)[:, :, 0:CW]
                san3 = s_san[:, :].rearrange("p (e w) -> p e w", w=CW)
                nv.tensor_scalar(san3, cmp3, 1.0, None, ALU.add)
                nv.drain()
                nv.tensor_tensor(s_san[:, :], s_san[:, :], s_msk[:, :],
                                 ALU.mult)
                nv.drain()
                nv.tensor_scalar(s_san[:, :], s_san[:, :], -1.0, None, ALU.add)
                nv.drain()
                nv.tensor_copy(s_idxa[0:16, :], s_san[:, :])
                nv.drain()
                ins = nv.tensor_copy(s_nfi[:, :], s_nff[:, :])
                S["s_vLst"].inc(ins, 1)
                lists_state["iv"] = S["s_vLst"].n
                S["s_vLst"].wait(nsync)
                for rr in range(1, 8):
                    ins = nsync.dma_start(s_idxa[16 * rr:16 * (rr + 1), :],
                                          s_idxa[0:16, :])
                    S["s_rep"].inc(ins, 16)
                lists_state["rv"] = S["s_rep"].n

            arr_ic = min(6, c.ICN - 1)
            sg_oi = max(0, min(2, c.OI - 3))
            post_oi = max(sg_oi, min(6, c.OI - 2))
            ga_oi = max(post_oi, min(c.OI - 1, post_oi + 1))

            S["s_dbg"].wait(nv)
            qlast = [None, None]
            pend_sc = []
            hval = {}
            yval = {}
            gav = {}
            gwv = {}
            scv = {}

            def flush_sc(upto_oi=None):
                while pend_sc and (upto_oi is None
                                   or pend_sc[0][0] <= upto_oi):
                    pend_sc.pop(0)[1]()

            def issue_gathers(e):
                ngp.wait_ge(S["s_rep"].h, lists_state["rv"])
                ngp.wait_ge(S["s_vLst"].h, lists_state["iv"])
                S["s_s1h"].wait(ngp)
                S["s_wh"].wait(ngp)
                if e >= 2:
                    ngp.wait_ge(S["s_tH"].h, hval[e - 2])
                    ngp.wait_ge(S["s_vY"].h, yval[e - 2])
                ngp.reg_load(rg2[e % 3], s_nfi[0:1, e:e + 1])
                sga = S["s_ga"] if e % 2 == 0 else S["s_ga1"]
                sgw = S["s_gw2"] if e % 2 == 0 else S["s_gw3"]
                ins = ngp.dma_gather(
                    gx[e % 2][:, :].rearrange("p (d t) -> p d t", t=c.CAP),
                    d_s1h, s_idxa[:, e * CW:(e + 1) * CW], c.CAP, rg2[e % 3],
                    c.D, transpose=True)
                sga.inc(ins, 16)
                gav[e] = (sga.h, sga.n)
                ins = ngp.dma_gather(
                    wg[e % 2][:, :].rearrange("p (t e2) -> p t e2", e2=64),
                    d_wh, s_idxa[:, e * CW:(e + 1) * CW], c.CAP, rg2[e % 3],
                    64, transpose=False)
                sgw.inc(ins, 16)
                gwv[e] = (sgw.h, sgw.n)


            def queue_scatter(e, q, vyv):
                qb = q % 2
                ssc = S["s_sc"] if e % 2 == 0 else S["s_sc1"]

                def emit(e=e, q=q, qb=qb, vyv=vyv, ssc=ssc):
                    S["s_yz"].wait(ngp)
                    ngp.wait_ge(S["s_vY"].h, vyv)
                    ysb3 = ysbq[qb][:, :].rearrange("p (t d) -> p t d",
                                                    d=W_SC)
                    ins = ngp.dma_scatter_add(
                        d_yh[:, q * W_SC:(q + 1) * W_SC], ysb3[:, :, :],
                        s_idxa[:, e * CW:(e + 1) * CW], c.CAP, rg2[e % 3],
                        W_SC, elem_step=c.D)
                    ssc.inc(ins, 16)
                    scv[(e, q)] = (ssc.h, ssc.n)
                    qlast[qb] = (ssc.h, ssc.n)

                emit_oi = min((q + 1) * OSC + 5, (q + 2) * OSC - 1)
                if q == N_SC - 1:
                    emit_oi = 10 ** 9
                pend_sc.append((emit_oi, emit))

            pre4 = {}
            for u in range(c.NU):
                sparse = u >= 2
                e = u - 2
                if sparse and e + 1 < c.NE:
                    issue_gathers(e + 1)
                S["s_tY"].wait(nv)
                tspan = c.CAP if sparse else 512
                t2n = 1 if sparse else c.T2N
                for t2 in range(t2n):
                    S["s_tY"].wait(nv)
                    pre2 = {}
                    for ic in range(c.ICN):
                        if u == 1 and t2 == t2n - 1 and ic == arr_ic:
                            hook_arr()
                        if ic == max(0, c.ICN - 3):
                            pre2[0] = dma_ring2(d_w2[u, 0])
                        if ic == c.ICN - 1 and 0 in pre2 and c.OI > 1:
                            pre2[1] = dma_ring2(d_w2[u, 1])
                        if t2 == 0 and ic == 0 and u in pre4:
                            i3, i1 = pre4.pop(u)
                        else:
                            i3 = dma_ring4(d_w3[u, ic])
                            i1 = dma_ring4(d_w1[u, ic])
                        j = t2 * c.ICN + ic
                        p2 = j % 2
                        bh1, bh3 = P[0 + 2 * p2], P[1 + 2 * p2]
                        if u > 0 or j >= 2:
                            nt.wait_ge(S["s_cSilu"].h, S["s_cSilu"].n - 1)
                            nt.wait_ge(S["s_vHp"].h, S["s_vHp"].n - 1)
                        if sparse and ic == 0:
                            nt.wait_ge(*gav[e])
                        s3 = use_ring4(i3)
                        for dc in range(c.DC):
                            mov = (gx[e % 2][:, dc * c.CAP:(dc + 1) * c.CAP]
                                   if sparse else
                                   mirror[:, dc * c.T + t2 * 512:
                                          dc * c.T + (t2 + 1) * 512])
                            nt.matmul(bh3[:, 0:tspan],
                                      s3[:, dc * 128:(dc + 1) * 128],
                                      mov, start=(dc == 0), stop=(dc == c.DC - 1))
                        s1 = use_ring4(i1)
                        for dc in range(c.DC):
                            mov = (gx[e % 2][:, dc * c.CAP:(dc + 1) * c.CAP]
                                   if sparse else
                                   mirror[:, dc * c.T + t2 * 512:
                                          dc * c.T + (t2 + 1) * 512])
                            ins = nt.matmul(bh1[:, 0:tspan],
                                            s1[:, dc * 128:(dc + 1) * 128],
                                            mov, start=(dc == 0),
                                            stop=(dc == c.DC - 1))
                        S["s_tH"].inc(ins, 1)
                        consumed4.append((S["s_tH"].h, S["s_tH"].n))
                        consumed4.append((S["s_tH"].h, S["s_tH"].n))
                        S["s_tH"].wait(nsc)
                        nsc.drain()
                        nsc.wait_ge(S["s_vHp"].h, max(0, S["s_vHp"].n - 1))
                        ins = nsc.activation(t1b[p2][:, 0:tspan],
                                             bh1[:, 0:tspan], ACTF.Silu)
                        S["s_cSilu"].inc(ins, 1)
                        S["s_cSilu"].wait(nv)
                        nv.drain()
                        ins = nv.tensor_tensor(
                            hp[:, ic * tspan:(ic + 1) * tspan],
                            t1b[p2][:, 0:tspan], bh3[:, 0:tspan], ALU.mult)
                        S["s_vHp"].inc(ins, 1)
                    if sparse:
                        hval[e] = S["s_tH"].n

                    ntok = tspan // 128
                    if t2 == t2n - 1:
                        flush_sc()
                    for oi in range(c.OI):
                        if u == 1 and t2 == t2n - 1:
                            if oi == sg_oi:
                                hook_sg()
                            if oi == post_oi:
                                hook_post()
                            if oi == ga_oi:
                                issue_gathers(0)
                        if sparse:
                            flush_sc(upto_oi=oi)
                        if (t2 == t2n - 1 and oi == max(0, c.OI - 4)
                                and u + 1 < c.NU):
                            pre4[u + 1] = (dma_ring4(d_w3[u + 1, 0]),
                                           dma_ring4(d_w1[u + 1, 0]))
                        si = pre2[oi] if oi in pre2 else dma_ring2(d_w2[u, oi])
                        slab = use_ring2(si)
                        S["s_vHp"].wait(nt)
                        for tch in range(ntok):
                            yb = P[5 + (tch % 2)]
                            nt.wait_ge(S["s_vY"].h, max(0, S["s_vY"].n - 1))
                            for ic in range(c.ICN):
                                ins = nt.matmul(
                                    yb[:, :],
                                    hp[:, ic * tspan + tch * 128:
                                       ic * tspan + (tch + 1) * 128],
                                    slab[:, ic * 512:(ic + 1) * 512],
                                    start=(ic == 0), stop=(ic == c.ICN - 1))
                            S["s_tY"].inc(ins, 1)
                            if tch == ntok - 1:
                                consumed2.append((S["s_tY"].h, S["s_tY"].n))
                            S["s_tY"].wait(nv)
                            nv.drain()
                            if not sparse:
                                tc = t2 * 4 + tch
                                msl = master[:, tc * c.D + oi * 512:
                                             tc * c.D + (oi + 1) * 512]
                                ins = nv.tensor_tensor(msl, msl, yb[:, :],
                                                       ALU.add)
                            else:
                                qb = (oi // OSC) % 2
                                nv.wait_ge(*gwv[e])
                                if (oi % OSC == 0 and tch == 0
                                        and qlast[qb] is not None):
                                    nv.wait_ge(*qlast[qb])
                                ins = nv.tensor_scalar(
                                    ysbq[qb][:, tch * W_SC
                                             + (oi % OSC) * 512:
                                             tch * W_SC
                                             + (oi % OSC) * 512 + 512],
                                    yb[:, :],
                                    wg[e % 2][:, tch * 64 + e: tch * 64 + e + 1],
                                    None, ALU.mult)
                            S["s_vY"].inc(ins, 1)
                        if sparse and (oi + 1) % OSC == 0:
                            queue_scatter(e, oi // OSC, S["s_vY"].n)
                if sparse:
                    yval[e] = S["s_vY"].n

            flush_sc()
            S["s_sc"].wait(nsc)
            S["s_sc1"].wait(nsc)
            S["s_tH"].wait(nsc)
            ybv = {}
            for tc in range(c.TC):
                ybk = gx[tc % 2]
                if tc >= 2:
                    nsc.wait_ge(S["s_vY"].h, ybv[tc - 2])
                syb = S["s_yb0"] if tc % 2 == 0 else S["s_yb1"]
                ins = nsc.dma_start(ybk[:, 0:c.D],
                                    d_yh[tc * 128:(tc + 1) * 128, :])
                syb.inc(ins, 16)
                syb.wait(nv)
                nv.drain()
                msl = master[:, tc * c.D:(tc + 1) * c.D]
                ins = nv.tensor_tensor(msl, msl, ybk[:, 0:c.D], ALU.add)
                S["s_vY"].inc(ins, 1)
                ybv[tc] = S["s_vY"].n

            if "wal" in d_dbg:
                S["s_vGate"].wait(nsync)
                ins = nsync.dma_start(d_dbg["wal"], w_all[:, :])
                S["s_dbg"].inc(ins, 16)
                S["s_dbg"].wait(nsync)
            if "hp" in d_dbg:
                S["s_vHp"].wait(nsync)
                ins = nsync.dma_start(d_dbg["hp"], hp[:, :])
                S["s_dbg"].inc(ins, 16)
                S["s_dbg"].wait(nsync)

            for tc in range(c.TC):
                msl = master[:, tc * c.D:(tc + 1) * c.D]
                nv.drain()
                for gi in range(c.D // 512):
                    nv.bn_stats(bnbuf[:, gi * 6:(gi + 1) * 6],
                                msl[:, gi * 512:(gi + 1) * 512])
                    nv.drain()
                nv.bn_aggr(bnmv[:, :], bnbuf[:, :])
                nv.drain()
                nv.tensor_scalar(red2[:, :], bnmv[:, 1:2], 1.0, EPS_LN,
                                 ALU.mult, ALU.add)
                nv.drain()
                ins = nv.reciprocal(rstd[:, :], red2[:, :])
                S["s_vLN"].inc(ins, 1)
                S["s_vLN"].wait(nsc)
                nsc.drain()
                ins = nsc.activation(rstd[:, :], rstd[:, :], ACTF.Sqrt)
                S["s_cStd"].inc(ins, 1)
                S["s_cStd"].wait(nv)
                nv.drain()
                ins = nv.tensor_scalar(msl, msl, bnmv[:, 0:1], rstd[:, :],
                                       ALU.subtract, ALU.mult)
                S["s_vLN"].inc(ins, 1)
                S["s_vLN"].wait(nsync)
                ins = nsync.dma_start(d_out[tc * 128:(tc + 1) * 128, :], msl)
                S["s_out"].inc(ins, 16)
            S["s_out"].wait(nsync)
            S["s_dbg"].wait(nsync)

    ffb.close()
    wob.close()
    es.close()
    return nc


def _tile_stationary(W):
    O, D = W.shape
    return np.ascontiguousarray(
        W.reshape(O // 128, 128, D // 128, 128).transpose(0, 3, 2, 1)
        .reshape(O // 128, 128, D))


def _tile_moving(W, nt):
    O, I = W.shape
    return np.ascontiguousarray(
        W.reshape(O // nt, nt, I // 128, 128).transpose(0, 3, 2, 1)
        .reshape(O // nt, 128, (I // 128) * nt))


def _rope_tables(positions, hd=128, base=ROPE_BASE):
    inv = (1.0 / base ** (np.arange(0, hd, 2, dtype=np.float32)
                          / np.float32(hd))).astype(np.float32)
    freqs = np.outer(positions.astype(np.float32), inv)
    emb = np.concatenate([freqs, freqs], axis=-1)
    cos = np.cos(emb).astype(np.float32)
    sin = np.sin(emb).astype(np.float32)
    sgn = np.where(np.arange(hd) < hd // 2, np.float32(-1.0), np.float32(1.0))
    return (np.ascontiguousarray(cos.T),
            np.ascontiguousarray((sin * sgn[None, :]).T))


def host_inputs(cfg, n_cores, x, Wq, Wk, Wv, Wo, gate_w, w1, w2, w3,
                sw1, sw2, sw3):
    c = cfg
    wq_st = _tile_stationary(np.asarray(Wq))
    wk_st = _tile_stationary(np.asarray(Wk))
    wv_st = _tile_stationary(np.asarray(Wv))
    wo_mv = _tile_moving(np.asarray(Wo), 512)
    gwT = np.ascontiguousarray(np.asarray(gate_w).T)
    gw_mv = np.ascontiguousarray(
        gwT.reshape(c.DC, 128, c.NE).transpose(1, 0, 2).reshape(128, c.DC * c.NE))
    ITR = c.ITR
    w1u = [np.asarray(sw1)[0:ITR], np.asarray(sw1)[ITR:2 * ITR]] + \
          [np.asarray(w1)[e] for e in range(c.NE)]
    w3u = [np.asarray(sw3)[0:ITR], np.asarray(sw3)[ITR:2 * ITR]] + \
          [np.asarray(w3)[e] for e in range(c.NE)]
    w2u = [np.asarray(sw2)[:, 0:ITR], np.asarray(sw2)[:, ITR:2 * ITR]] + \
          [np.asarray(w2)[e] for e in range(c.NE)]
    w1_st = np.ascontiguousarray(np.stack([_tile_stationary(w) for w in w1u]))
    w3_st = np.ascontiguousarray(np.stack([_tile_stationary(w) for w in w3u]))
    w2_mv = np.ascontiguousarray(np.stack([_tile_moving(w, 512) for w in w2u]))
    maskT = np.ascontiguousarray(
        np.tile(np.tril(np.ones((128, 128), np.float32)).T, (1, c.R)))
    iotac = np.ascontiguousarray(np.tile(
        np.arange(c.CAP, dtype=np.float32).reshape(c.CAP // 16, 16).T,
        (1, c.NE)))
    on16 = np.ones((1, 16), np.float32)
    tokp1 = np.zeros((16, c.T // 16), np.float32)
    for jj in range(8):
        for tc in range(c.TC):
            tokp1[:, jj * c.TC + tc] = (tc * 128 + jj * 16 +
                                        np.arange(16) + 1.0)
    tokp1 = np.ascontiguousarray(np.tile(tokp1, (1, c.NE)))
    wh0 = np.zeros((c.T, 64), np.float32)
    identb = np.eye(128, dtype=np.float32).astype(BF)
    ones1 = np.ones((1, 128), np.float32)

    xf = np.asarray(x)
    B = xf.shape[0]
    halves = n_cores // B
    in_maps = []
    for core in range(n_cores):
        b, h = core // halves, core % halves
        x_own = np.ascontiguousarray(xf[b, h * c.T:(h + 1) * c.T, :])
        if h > 0:
            x_pre = np.ascontiguousarray(xf[b, (h - 1) * c.T:h * c.T, :]) \
                if c.PRE == c.T else np.ascontiguousarray(xf[b, 0:c.PRE, :])
            pre_flag = 1.0
        else:
            x_pre = np.zeros((c.PRE, c.D), np.float32)
            pre_flag = 0.0
        pos = np.zeros(c.NB * 512, np.float32)
        for blk in range(c.NB):
            if blk < c.NBP:
                pos[blk * 512:(blk + 1) * 512] = np.arange(blk * 512,
                                                           (blk + 1) * 512)
            else:
                off = h * c.T + (blk - c.NBP) * 512
                pos[blk * 512:(blk + 1) * 512] = np.arange(off, off + 512)
        ctab, stab = _rope_tables(pos, c.HD)
        vcol = np.zeros((128, c.NB), np.float32)
        vcol[:, 0:c.NBP] = pre_flag
        vcol[:, c.NBP:] = 1.0
        in_maps.append({
            "x_own": x_own, "x_pre": x_pre, "ctab": ctab, "stab": stab,
            "maskT": maskT, "identb": identb, "ones1": ones1, "vcol": vcol,
            "wq_st": wq_st, "wk_st": wk_st, "wv_st": wv_st, "wo_mv": wo_mv,
            "gw_mv": gw_mv, "w1_st": w1_st, "w3_st": w3_st, "w2_mv": w2_mv,
            "iotac": iotac, "on16": on16, "tokp1": tokp1, "wh": wh0,
        })
    return in_maps


def kernel(L0_attn_x, L0_attn_Wq, L0_attn_Wk, L0_attn_Wv, L0_attn_Wo,
           L0_ffn_gate_w, L0_ffn_w1, L0_ffn_w2, L0_ffn_w3,
           L0_ffn_sw1, L0_ffn_sw2, L0_ffn_sw3):
    cfg = Cfg()
    in_maps = host_inputs(cfg, 8, L0_attn_x, L0_attn_Wq, L0_attn_Wk,
                          L0_attn_Wv, L0_attn_Wo, L0_ffn_gate_w, L0_ffn_w1,
                          L0_ffn_w2, L0_ffn_w3, L0_ffn_sw1, L0_ffn_sw2,
                          L0_ffn_sw3)
    nc = build_nc(cfg)
    mybir.codegen_inst_isa_subclasses(nc)
    res = run_bass_kernel_spmd(nc, in_maps, core_ids=list(range(8)))
    out = np.empty((4, 2048, 2048), np.float32)
    for core in range(8):
        b, h = core // 2, core % 2
        out[b, h * cfg.T:(h + 1) * cfg.T, :] = res.results[core]["out"]
    return out
